# version 03 (final)
# baseline (speedup 1.0000x reference)
import contextlib
import numpy as np
import concourse.bass as bass
import concourse.mybir as mybir
from concourse.bass_utils import run_bass_kernel_spmd

F32 = mybir.dt.float32
BF16 = mybir.dt.bfloat16
I32 = mybir.dt.int32
AF = mybir.ActivationFunctionType
ALU = mybir.AluOpType
AX = mybir.AxisListType

PART = 128
D = 1024
NCHUNK = D // PART
S = 2048
B = 16
NCORES = 8
BPC = B // NCORES
HD = 64
TWO_PI = float(2 * np.pi)
BIG_NEG = -1.0e30

ENGINES = ("sync", "tensor", "vector", "scalar", "gpsimd")


class Rec:
    __slots__ = ("meth", "args", "kwargs", "incs")

    def __init__(self, meth, args, kwargs):
        self.meth = meth
        self.args = args
        self.kwargs = kwargs
        self.incs = []

    def then_inc(self, sem, n=1):
        sem.n += n
        self.incs.append((sem, n))
        return self


class EngineProxy:
    def __init__(self, prog, name):
        self._prog = prog
        self._name = name

    def __getattr__(self, meth):
        prog, name = self._prog, self._name

        def rec(*args, **kwargs):
            r = Rec(meth, args, kwargs)
            prog.q[name].append(r)
            return r

        return rec

    def wait(self, sem, val=None):
        v = sem.n if val is None else val
        if v > 0:
            r = Rec("wait_ge", (sem, v), {})
            self._prog.q[self._name].append(r)

    def last(self):
        return self._prog.q[self._name][-1]

    def fence(self, sem):
        lastr = self._prog.q[self._name][-1]
        if lastr.meth in ("wait_ge", "sem_inc"):
            return
        lastr.then_inc(sem, 1)
        self.wait(sem)


class Sem:
    __slots__ = ("name", "h", "n")

    def __init__(self, name):
        self.name = name
        self.h = None
        self.n = 0


class Prog:
    def __init__(self, nc):
        self.nc = nc
        self.q = {e: [] for e in ENGINES}
        self.sems = {}
        self.sync = EngineProxy(self, "sync")
        self.tensor = EngineProxy(self, "tensor")
        self.vector = EngineProxy(self, "vector")
        self.scalar = EngineProxy(self, "scalar")
        self.gpsimd = EngineProxy(self, "gpsimd")

    def eng(self, name):
        return getattr(self, name)

    def sem(self, name):
        if name not in self.sems:
            self.sems[name] = Sem(name)
        return self.sems[name]

    def barrier(self, tag=None):
        s = self.sem("barrier")
        for e in ENGINES:
            q = self.q[e]
            if q and q[-1].meth not in ("wait_ge", "sem_inc"):
                q[-1].then_inc(s, 1)
            else:
                r = Rec("sem_inc", (s, 1), {})
                q.append(r)
                s.n += 1
        target = s.n
        for e in ENGINES:
            self.eng(e).wait(s, target)

    def build(self):
        nc = self.nc
        with contextlib.ExitStack() as st:
            for s in self.sems.values():
                s.h = st.enter_context(nc.semaphore(s.name))
            with nc.Block() as block:
                for ename in ENGINES:
                    ops = self.q[ename]

                    def make(ops):
                        def body(e):
                            nwait = 0
                            for r in ops:
                                if r.meth == "wait_ge":
                                    if nwait >= 2:
                                        e.nop(nofuse=True)
                                        nwait = 0
                                    sem, val = r.args
                                    e.wait_ge(sem.h, val)
                                    nwait += 1
                                    continue
                                nwait = 0
                                if r.meth == "sem_inc":
                                    sem, v = r.args
                                    e.sem_inc(sem.h, v)
                                    continue
                                ins = getattr(e, r.meth)(*r.args, **r.kwargs)
                                for idx, (sem, v) in enumerate(r.incs):
                                    if idx == 0:
                                        ins.then_inc(sem.h, v)
                                    else:
                                        e.drain().then_inc(sem.h, v)
                        return body

                    getattr(block, ename)(make(ops))
        return nc


def _host_constants():
    import ml_dtypes
    c = {}
    c["identity_bf16"] = np.eye(PART).astype(ml_dtypes.bfloat16)
    c["iota8t"] = np.tile(np.arange(8, dtype=np.float32), (PART, 16))
    for ri, r in enumerate((0, 128, 256, 384)):
        m = np.zeros((PART, 512), dtype=np.float32)
        j = np.arange(512)[None, :]
        p = np.arange(PART)[:, None]
        m[j > (r + p)] = BIG_NEG
        c[f"maskadd_{ri}"] = m
    return c


_CONSTS = _host_constants()


class Kb:
    def __init__(self, nc):
        self.nc = nc
        self.p = Prog(nc)
        self.stack = contextlib.ExitStack()
        self.inputs = {}
        self._uid = 0

    def uid(self):
        self._uid += 1
        return self._uid

    def dram_in(self, name, shape, dtype=F32):
        return self.nc.dram_tensor(name, list(shape), dtype, kind="ExternalInput").ap()

    def dram_out(self, name, shape, dtype=F32):
        return self.nc.dram_tensor(name, list(shape), dtype, kind="ExternalOutput").ap()

    def dram_tmp(self, name, shape, dtype=F32):
        return self.nc.dram_tensor(name, list(shape), dtype, kind="Internal").ap()

    def sbuf(self, name, shape, dtype=F32, stack=None):
        return (stack or self.stack).enter_context(self.nc.sbuf_tensor(name, shape, dtype))

    def psum(self, name, shape, dtype=F32, stack=None):
        return (stack or self.stack).enter_context(self.nc.psum_tensor(name, shape, dtype))


def build_common(kb):
    p = kb.p
    kb.ones_col = kb.sbuf("ones_col", [PART, 1], F32)
    kb.ones_row = kb.sbuf("ones_row", [1, PART], F32)
    kb.eps_col = kb.sbuf("eps_col", [PART, 1], F32)
    kb.id_bf = kb.sbuf("id_bf", [PART, PART], BF16)
    s = p.sem("init")
    p.vector.memset(kb.ones_col[:], 1.0)
    p.vector.memset(kb.ones_row[:], 1.0)
    p.vector.memset(kb.eps_col[:], 1e-5).then_inc(s, 1)
    id_in = kb.dram_in("c_id_bf", [PART, PART], BF16)
    kb.inputs["c_id_bf"] = _CONSTS["identity_bf16"]
    p.sync.dma_start(out=kb.id_bf[:], in_=id_in).then_inc(s, 16)
    kb.init_sem = s


def build_l0_attn(kb, x_hbm_list, y_hbm_list):
    nc, p = kb.nc, kb.p
    sb, ps = kb.sbuf, kb.psum

    wqkv = kb.dram_in("L0_attn_W_qkv", [D, 3 * D])
    wproj = kb.dram_in("L0_attn_W_proj", [D, D])
    u_qkv = kb.dram_in("L0_attn_u_qkv_r", [1, 3 * D])
    u_proj = kb.dram_in("L0_attn_u_proj_r", [1, D])
    s_qkv_h = kb.dram_in("L0_attn_sigma_qkv_r", [1, 1])
    s_proj_h = kb.dram_in("L0_attn_sigma_proj_r", [1, 1])
    masks = [kb.dram_in(f"c_maskadd_{ri}", [PART, 512]) for ri in range(4)]
    for ri in range(4):
        kb.inputs[f"c_maskadd_{ri}"] = _CONSTS[f"maskadd_{ri}"]

    mask_sb = [sb(f"mask{ri}", [PART, 512], F32) for ri in range(4)]
    wp_bf = sb("l0a_wpbf", [PART, NCHUNK * D], BF16)
    cq_col = sb("l0a_cqcol", [PART, 1], F32)
    cp_col = sb("l0a_cpcol", [PART, 1], F32)
    sc_t = sb("l0a_sct", [1, 16], F32)

    psA = ps("l0a_psA", [PART, 2048], F32)
    psB = ps("l0a_psB", [PART, 2048], BF16)
    psC = ps("l0a_psC", [PART, 512], F32)
    psD = ps("l0a_psD", [PART, 512], F32)

    s_x = p.sem("l0a_x")
    s_w = p.sem("l0a_w")
    s_mm = p.sem("l0a_mm")
    s_ev = p.sem("l0a_ev")
    s_ve = p.sem("l0a_ve")
    s_ac = p.sem("l0a_ac")
    s_out = p.sem("l0a_out")

    for ri in range(4):
        p.sync.dma_start(out=mask_sb[ri][:], in_=masks[ri]).then_inc(s_x, 16)

    with contextlib.ExitStack() as sig_stack:
        u_b = kb.sbuf("l0a_ub", [PART, 3 * D], F32, stack=sig_stack)
        wf32 = [kb.sbuf(f"l0a_wf{i}", [PART, 3 * D], F32, stack=sig_stack) for i in range(2)]
        tmp_w = kb.sbuf("l0a_tmpw", [PART, 3 * D], F32, stack=sig_stack)
        v_col = kb.sbuf("l0a_vcol", [PART, NCHUNK], F32, stack=sig_stack)
        vn_col = kb.sbuf("l0a_vncol", [PART, NCHUNK], F32, stack=sig_stack)
        wv_row = kb.sbuf("l0a_wvrow", [1, 3 * D], F32, stack=sig_stack)
        rb_col = kb.sbuf("l0a_rbcol", [PART, 1], F32, stack=sig_stack)

        def sigma_phase(W_hbm, u_hbm, sp_hbm, ncols, out_col, keep_bf):
            ntile = ncols // 512
            u_sb = kb.sbuf(f"l0a_usb{kb.uid()}", [1, ncols], F32, stack=sig_stack)
            sp_sb = kb.sbuf(f"l0a_spsb{kb.uid()}", [1, 1], F32, stack=sig_stack)
            p.sync.dma_start(out=u_sb[:], in_=u_hbm).then_inc(s_x, 16)
            p.sync.dma_start(out=sp_sb[:], in_=sp_hbm).then_inc(s_x, 16)
            p.tensor.wait(s_x)
            p.tensor.wait(kb.init_sem)
            p.scalar.wait(kb.init_sem)
            p.vector.wait(kb.init_sem)
            p.tensor.wait(s_ev)
            for j in range(ntile):
                p.tensor.matmul(psD[:, 0:512], kb.ones_row[:], u_sb[0:1, j * 512:(j + 1) * 512],
                                start=True, stop=True).then_inc(s_mm, 1)
                p.vector.wait(s_mm)
                p.vector.tensor_copy(out=u_b[:, j * 512:(j + 1) * 512], in_=psD[:, 0:512]).then_inc(s_ev, 1)
                p.tensor.wait(s_ev)
            p.sync.wait(s_ve)
            p.sync.wait(s_mm)
            pass1_done = []
            for c in range(NCHUNK):
                buf = wf32[c % 2]
                if c >= 2:
                    p.sync.wait(s_ve, pass1_done[c - 2])
                p.sync.dma_start(out=buf[:, 0:ncols],
                                 in_=W_hbm[c * PART:(c + 1) * PART, :]).then_inc(s_w, 16)
                p.vector.wait(s_w)
                p.vector.tensor_tensor(out=tmp_w[:, 0:ncols], in0=buf[:, 0:ncols],
                                       in1=u_b[:, 0:ncols], op=ALU.mult)
                p.vector.reduce_sum(out=v_col[:, c:c + 1], in_=tmp_w[:, 0:ncols], axis=AX.X)
                if keep_bf is not None:
                    p.vector.tensor_copy(out=keep_bf[:, c * ncols:(c + 1) * ncols], in_=buf[:, 0:ncols])
                p.vector.last().then_inc(s_ve, 1)
                pass1_done.append(s_ve.n)
            p.scalar.wait(s_ve)
            p.scalar.activation(out=vn_col[:, 0:NCHUNK], in_=v_col[:, 0:NCHUNK], func=AF.Square).then_inc(s_ac, 1)
            p.tensor.wait(s_ac)
            p.tensor.matmul(psD[0:1, 0:NCHUNK], kb.ones_col[:], vn_col[:, 0:NCHUNK],
                            start=True, stop=True).then_inc(s_mm, 1)
            p.vector.wait(s_mm)
            p.vector.reduce_sum(out=sc_t[0:1, 0:1], in_=psD[0:1, 0:NCHUNK], axis=AX.X).then_inc(s_ev, 1)
            p.scalar.wait(s_ev)
            p.scalar.activation(out=sc_t[0:1, 1:2], in_=sc_t[0:1, 0:1], func=AF.Ln)
            p.scalar.activation(out=sc_t[0:1, 2:3], in_=sc_t[0:1, 1:2], func=AF.Exp, scale=-0.5).then_inc(s_ac, 1)
            p.tensor.wait(s_ac)
            p.tensor.matmul(psD[:, 16:17], kb.ones_row[:], sc_t[0:1, 2:3], start=True, stop=True).then_inc(s_mm, 1)
            p.vector.wait(s_mm)
            p.vector.tensor_copy(out=rb_col[:], in_=psD[:, 16:17]).then_inc(s_ve, 1)
            p.vector.wait(s_ve)
            p.vector.tensor_scalar(out=vn_col[:, 0:NCHUNK], in0=v_col[:, 0:NCHUNK],
                                   scalar1=rb_col[:], scalar2=None, op0=ALU.mult).then_inc(s_ve, 1)
            slots = [psA[0:1, j * 512:(j + 1) * 512] for j in range(4)] + [psC[0:1, 0:512], psD[0:1, 0:512]]
            p.tensor.wait(s_ve)
            p.tensor.wait(s_ev)
            p.sync.wait(s_ve)
            pass2_marks = []
            for c in range(NCHUNK):
                buf = wf32[c % 2]
                if c >= 2:
                    p.sync.wait(s_mm, pass2_marks[c - 2])
                p.sync.dma_start(out=buf[:, 0:ncols],
                                 in_=W_hbm[c * PART:(c + 1) * PART, :]).then_inc(s_w, 16)
                p.tensor.wait(s_w)
                for j in range(ntile):
                    p.tensor.matmul(slots[j], vn_col[:, c:c + 1],
                                    buf[:, j * 512:(j + 1) * 512],
                                    start=(c == 0), stop=(c == NCHUNK - 1))
                p.tensor.last().then_inc(s_mm, 1)
                pass2_marks.append(s_mm.n)
            p.vector.wait(s_mm)
            for j in range(ntile):
                p.vector.tensor_copy(out=wv_row[0:1, j * 512:(j + 1) * 512], in_=slots[j])
            p.vector.last().then_inc(s_ve, 1)
            p.scalar.wait(s_ve)
            p.scalar.activation(out=tmp_w[0:1, 0:ncols], in_=wv_row[0:1, 0:ncols], func=AF.Square).then_inc(s_ac, 1)
            p.vector.wait(s_ac)
            p.vector.reduce_sum(out=sc_t[0:1, 3:4], in_=tmp_w[0:1, 0:ncols], axis=AX.X).then_inc(s_ve, 1)
            p.scalar.wait(s_ve)
            p.scalar.wait(s_x)
            p.scalar.activation(out=sc_t[0:1, 4:5], in_=sc_t[0:1, 3:4], func=AF.Ln)
            p.scalar.activation(out=sc_t[0:1, 5:6], in_=sc_t[0:1, 4:5], func=AF.Exp, scale=-0.5).then_inc(s_ac, 1)
            p.vector.wait(s_ac)
            p.vector.wait(s_x)
            p.vector.tensor_tensor(out=sc_t[0:1, 6:7], in0=sc_t[0:1, 5:6], in1=sp_sb[0:1, 0:1],
                                   op=ALU.mult).then_inc(s_ve, 1)
            p.tensor.wait(s_ve)
            p.tensor.wait(s_ev)
            p.tensor.matmul(psD[:, 16:17], kb.ones_row[:], sc_t[0:1, 6:7], start=True, stop=True).then_inc(s_mm, 1)
            p.vector.wait(s_mm)
            p.vector.tensor_copy(out=out_col[:], in_=psD[:, 16:17]).then_inc(s_ev, 1)

        sigma_phase(wqkv, u_qkv, s_qkv_h, 3 * D, cq_col, None)
        sigma_phase(wproj, u_proj, s_proj_h, D, cp_col, wp_bf)
        p.barrier("l0a_sigma")

    X = [sb(f"l0a_x{c}", [PART, S], F32) for c in range(NCHUNK)]
    LNX = sb("l0a_lnx", [PART, NCHUNK * S], BF16)
    O = sb("l0a_o", [PART, NCHUNK * S], BF16)
    wq_g = [sb(f"l0a_wqg{i}", [PART, 3 * PART * NCHUNK], BF16) for i in range(2)]
    qg = sb("l0a_qg", [PART, S], BF16)
    kg = sb("l0a_kg", [PART, S], BF16)
    vg = sb("l0a_vg", [PART, S], BF16)
    vtok = [sb(f"l0a_vtok{hh}", [PART, 16 * HD], BF16) for hh in range(2)]
    pt_sb = sb("l0a_pt", [PART, S], BF16)
    pexp = sb("l0a_pexp", [PART, S], BF16)
    sdiag = sb("l0a_sdiag", [PART, 512], F32)
    m_col = sb("l0a_mcol", [PART, 1], F32)
    nb_col = sb("l0a_nbcol", [PART, 1], F32)
    rs_col = sb("l0a_rscol", [PART, 1], F32)
    rs2_col = sb("l0a_rs2col", [PART, 1], F32)
    r_col = sb("l0a_rcol", [PART, 1], F32)
    diag_b = sb("l0a_diag", [PART, PART], BF16)
    sq_t = sb("l0a_sq0", [PART, 512], F32)
    row_m = sb("l0a_rowm", [1, 512], F32)
    row_s2 = sb("l0a_rows2", [1, 512], F32)
    row_ln = sb("l0a_rowln", [1, 512], F32)
    row_r = sb("l0a_rowr", [1, 512], F32)
    row_nmr = sb("l0a_rownmr", [1, 512], F32)

    for b in range(BPC):
        x_hbm = x_hbm_list[b]
        y_hbm = y_hbm_list[b]
        p.sync.wait(s_out)
        for c in range(NCHUNK):
            p.sync.dma_start(out=X[c][:], in_=x_hbm[c * PART:(c + 1) * PART, :]).then_inc(s_x, 16)
        x_loaded = s_x.n

        p.tensor.wait(s_x, x_loaded)
        p.tensor.wait(s_ve)
        p.tensor.wait(s_ev)
        p.tensor.wait(s_ac)
        for t in range(4):
            sp = slice(t * 512, (t + 1) * 512)
            for c in range(NCHUNK):
                p.tensor.matmul(psD[0:1, 0:512], kb.ones_col[:], X[c][:, sp],
                                start=(c == 0), stop=(c == NCHUNK - 1))
            p.tensor.last().then_inc(s_mm, 1)
            p.scalar.wait(s_mm)
            p.scalar.activation(out=row_m[0:1, :], in_=psD[0:1, 0:512], func=AF.Copy,
                                scale=1.0 / D).then_inc(s_ac, 1)
            for c in range(NCHUNK):
                p.scalar.activation(out=sq_t[:], in_=X[c][:, sp], func=AF.Square).then_inc(s_ac, 1)
                p.tensor.wait(s_ac)
                p.tensor.matmul(psC[0:1, 0:512], kb.ones_col[:], sq_t[:],
                                start=(c == 0), stop=(c == NCHUNK - 1))
                if c < NCHUNK - 1:
                    p.tensor.last().then_inc(s_mm, 1)
                    p.scalar.wait(s_mm)
            p.tensor.last().then_inc(s_mm, 1)
            p.scalar.wait(s_mm)
            p.scalar.activation(out=row_s2[0:1, :], in_=psC[0:1, 0:512], func=AF.Copy, scale=1.0 / D)
            p.scalar.activation(out=row_ln[0:1, :], in_=row_m[0:1, :], func=AF.Square).then_inc(s_ac, 1)
            p.vector.wait(s_ac)
            p.vector.tensor_tensor(out=row_s2[0:1, :], in0=row_s2[0:1, :], in1=row_ln[0:1, :],
                                   op=ALU.subtract).then_inc(s_ve, 1)
            p.scalar.wait(s_ve)
            p.scalar.activation(out=row_ln[0:1, :], in_=row_s2[0:1, :], func=AF.Ln, bias=kb.eps_col[0:1, :])
            p.scalar.activation(out=row_r[0:1, :], in_=row_ln[0:1, :], func=AF.Exp, scale=-0.5).then_inc(s_ac, 1)
            p.vector.wait(s_ac)
            p.vector.scalar_tensor_tensor(out=row_nmr[0:1, :], in0=row_m[0:1, :], scalar=-1.0,
                                          in1=row_r[0:1, :], op0=ALU.mult, op1=ALU.mult).then_inc(s_ve, 1)
            p.tensor.wait(s_ve)
            p.tensor.matmul(psC[:, 0:512], kb.ones_row[:], row_r[0:1, :], start=True, stop=True)
            p.tensor.matmul(psD[:, 0:512], kb.ones_row[:], row_nmr[0:1, :],
                            start=True, stop=True).then_inc(s_mm, 1)
            p.vector.wait(s_mm)
            for c in range(NCHUNK):
                p.vector.tensor_tensor(out=sq_t[:], in0=X[c][:, sp], in1=psC[:, 0:512],
                                       op=ALU.mult)
                p.vector.tensor_tensor(out=LNX[:, c * S + t * 512: c * S + (t + 1) * 512],
                                       in0=sq_t[:], in1=psD[:, 0:512], op=ALU.add)
            p.vector.last().then_inc(s_ve, 1)
            p.tensor.wait(s_ve)
        ln_done_ve = s_ve.n
        p.barrier(f"l0a{b}_ln")

        wq_marks = [0, 0]
        for g in range(NCHUNK):
            wbuf = wq_g[g % 2]
            if wq_marks[g % 2]:
                p.gpsimd.wait(s_mm, wq_marks[g % 2])
            for c in range(NCHUNK):
                src = wqkv[c * PART:(c + 1) * PART, :] \
                    .rearrange("p (t n) -> p t n", t=3)[:, :, g * PART:(g + 1) * PART]
                p.gpsimd.dma_start(out=wbuf[:, c * 3 * PART:(c + 1) * 3 * PART]
                                   .rearrange("p (t n) -> p t n", t=3), in_=src).then_inc(s_w, 16)
            w_ready = s_w.n

            p.tensor.wait(s_w, w_ready)
            p.tensor.wait(s_ve, ln_done_ve)
            p.tensor.wait(s_ac)
            p.tensor.wait(s_ev)
            for t3, dst in ((0, qg), (1, kg), (2, vg)):
                for t in range(4):
                    for c in range(NCHUNK):
                        lhsT = wbuf[:, c * 3 * PART + t3 * PART: c * 3 * PART + (t3 + 1) * PART]
                        p.tensor.matmul(psA[:, t * 512:(t + 1) * 512], lhsT,
                                        LNX[:, c * S + t * 512:c * S + (t + 1) * 512],
                                        start=(c == 0), stop=(c == NCHUNK - 1))
                    p.tensor.last().then_inc(s_mm, 1)
                    p.vector.wait(s_mm)
                    p.vector.tensor_scalar(out=dst[:, t * 512:(t + 1) * 512],
                                           in0=psA[:, t * 512:(t + 1) * 512],
                                           scalar1=cq_col[:], scalar2=None,
                                           op0=ALU.mult).then_inc(s_ev, 1)
                p.tensor.wait(s_ev)
            wq_marks[g % 2] = s_mm.n

            for hh in range(2):
                off = hh * HD
                for kt in range(16):
                    p.tensor.transpose(psB[:, hh * 1024 + kt * HD: hh * 1024 + (kt + 1) * HD],
                                       vg[off:off + HD, kt * PART:(kt + 1) * PART],
                                       kb.id_bf[off:off + HD, off:off + HD])
                p.tensor.last().then_inc(s_mm, 1)
                p.vector.wait(s_mm)
                p.vector.tensor_copy(out=vtok[hh][:], in_=psB[:, hh * 1024:(hh + 1) * 1024]).then_inc(s_ev, 1)

            for hh in range(2):
                off = hh * HD
                for qt in range(16):
                    q0 = qt * PART
                    lk = q0 + PART
                    nfull = q0 // 512
                    dstart = nfull * 512
                    dw = lk - dstart
                    ri = (q0 % 512) // PART
                    p.tensor.wait(s_ac)
                    p.tensor.wait(s_ev)
                    for kbi in range(nfull):
                        p.tensor.matmul(psA[:, kbi * 512:(kbi + 1) * 512],
                                        qg[off:off + HD, q0:q0 + PART],
                                        kg[off:off + HD, kbi * 512:(kbi + 1) * 512],
                                        start=True, stop=True)
                    p.tensor.matmul(psA[:, dstart:dstart + dw],
                                    qg[off:off + HD, q0:q0 + PART],
                                    kg[off:off + HD, dstart:dstart + dw],
                                    start=True, stop=True).then_inc(s_mm, 1)
                    p.vector.wait(s_mm)
                    p.vector.tensor_tensor(out=sdiag[:, 0:dw], in0=psA[:, dstart:dstart + dw],
                                           in1=mask_sb[ri][:, 0:dw], op=ALU.add)
                    if nfull > 0:
                        p.vector.reduce_max(out=m_col[:], in_=psA[:, 0:dstart], axis=AX.X)
                        p.vector.reduce_max(out=nb_col[:], in_=sdiag[:, 0:dw], axis=AX.X)
                        p.vector.tensor_tensor(out=m_col[:], in0=m_col[:], in1=nb_col[:], op=ALU.max)
                    else:
                        p.vector.reduce_max(out=m_col[:], in_=sdiag[:, 0:dw], axis=AX.X)
                    p.vector.tensor_scalar(out=nb_col[:], in0=m_col[:], scalar1=-0.125, scalar2=None,
                                           op0=ALU.mult).then_inc(s_ve, 1)
                    p.scalar.wait(s_ve)
                    if nfull > 0:
                        p.scalar.activation(out=pexp[:, 0:dstart], in_=psA[:, 0:dstart], func=AF.Exp,
                                            scale=0.125, bias=nb_col[:], accum_out=rs_col[:])
                        p.scalar.activation(out=pexp[:, dstart:lk], in_=sdiag[:, 0:dw], func=AF.Exp,
                                            scale=0.125, bias=nb_col[:], accum_out=rs2_col[:]).then_inc(s_ac, 1)
                        p.vector.wait(s_ac)
                        p.vector.tensor_tensor(out=rs_col[:], in0=rs_col[:], in1=rs2_col[:], op=ALU.add)
                    else:
                        p.scalar.activation(out=pexp[:, dstart:lk], in_=sdiag[:, 0:dw], func=AF.Exp,
                                            scale=0.125, bias=nb_col[:], accum_out=rs_col[:]).then_inc(s_ac, 1)
                        p.vector.wait(s_ac)
                    p.vector.reciprocal(out=r_col[:], in_=rs_col[:]).then_inc(s_ve, 1)
                    p.vector.wait(s_ve)
                    p.vector.tensor_scalar(out=diag_b[:], in0=kb.id_bf[:], scalar1=r_col[:],
                                           scalar2=None, op0=ALU.mult).then_inc(s_ve, 1)
                    p.tensor.wait(s_ve)
                    nkt = qt + 1
                    for kt in range(nkt):
                        p.tensor.matmul(psA[:, kt * PART:(kt + 1) * PART],
                                        pexp[:, kt * PART:(kt + 1) * PART], diag_b[:],
                                        start=True, stop=True)
                    p.tensor.last().then_inc(s_mm, 1)
                    p.vector.wait(s_mm)
                    p.vector.tensor_copy(out=pt_sb[:, 0:nkt * PART], in_=psA[:, 0:nkt * PART]).then_inc(s_ev, 1)
                    p.tensor.wait(s_ev)
                    qslot = (qt % 4) * PART
                    for kt in range(nkt):
                        p.tensor.matmul(psC[0:HD, qslot:qslot + PART],
                                        vtok[hh][:, kt * HD:(kt + 1) * HD],
                                        pt_sb[:, kt * PART:(kt + 1) * PART],
                                        start=(kt == 0), stop=(kt == nkt - 1))
                    p.tensor.last().then_inc(s_mm, 1)
                    if qt % 4 == 3:
                        p.scalar.wait(s_mm)
                        p.scalar.copy(out=O[off:off + HD, g * S + (qt - 3) * PART: g * S + (qt + 1) * PART],
                                      in_=psC[0:HD, 0:512]).then_inc(s_ev, 1)
                        p.tensor.wait(s_ev)

        p.barrier(f"l0a{b}_att")
        p.tensor.wait(s_ac)
        for oc in range(NCHUNK):
            for t in range(4):
                sp = slice(t * 512, (t + 1) * 512)
                p.tensor.wait(s_ev)
                for ic in range(NCHUNK):
                    lhsT = wp_bf[:, ic * D + oc * PART: ic * D + (oc + 1) * PART]
                    p.tensor.matmul(psA[:, t * 512:(t + 1) * 512], lhsT,
                                    O[:, ic * S + t * 512: ic * S + (t + 1) * 512],
                                    start=(ic == 0), stop=(ic == NCHUNK - 1))
                p.tensor.last().then_inc(s_mm, 1)
                p.vector.wait(s_mm)
                p.vector.scalar_tensor_tensor(out=X[oc][:, sp], in0=psA[:, t * 512:(t + 1) * 512],
                                              scalar=cp_col[:], in1=X[oc][:, sp],
                                              op0=ALU.mult, op1=ALU.add).then_inc(s_ev, 1)
        p.sync.wait(s_ev)
        for c in range(NCHUNK):
            p.sync.dma_start(out=y_hbm[c * PART:(c + 1) * PART, :], in_=X[c][:]).then_inc(s_out, 16)
        p.barrier(f"l0a{b}_end")
    if getattr(kb, "debug", False):
        dbg = {}
        dbg["d_cq"] = (cq_col, [PART, 1], F32)
        dbg["d_cp"] = (cp_col, [PART, 1], F32)
        dbg["d_sct"] = (sc_t, [1, 16], F32)
        dbg["d_lnx"] = (LNX, [PART, NCHUNK * S], BF16)
        dbg["d_o"] = (O, [PART, NCHUNK * S], BF16)
        dbg["d_qg"] = (qg, [PART, S], BF16)
        dbg["d_kg"] = (kg, [PART, S], BF16)
        dbg["d_pexp"] = (pexp, [PART, S], BF16)
        kb.dbg_names = list(dbg)
        for nm, (t, shp, dt) in dbg.items():
            hbm = kb.dram_out(nm, shp, dt)
            p.sync.dma_start(out=hbm, in_=t[:]).then_inc(s_out, 16)
    p.sync.wait(s_out)




def build_l0_ffn(kb, x_hbm_list, y_hbm_list):
    nc, p = kb.nc, kb.p
    sb, ps = kb.sbuf, kb.psum

    wpwT = kb.dram_in("L0f_pwT", [D, D])
    wewT = kb.dram_in("L0f_ewT", [8, D, D])
    wowT = kb.dram_in("L0f_owT", [D, D])
    wffdT = kb.dram_in("L0f_ffdT", [D, 256])
    wffuT = kb.dram_in("L0f_ffuT", [256, D])
    prm = kb.dram_in("L0f_prm", [PART, 24 + 8 + 8 + 64 + 8 + 8 + 2 + 8])
    rb_h = kb.dram_in("L0f_rb", [1, 8])
    rwT_h = kb.dram_in("L0f_rwT", [D, 8])
    sel8_h = kb.dram_in("c_sel8", [8, D], BF16)
    idf_h = kb.dram_in("c_id_f32", [PART, PART])
    iota_h = kb.dram_in("c_iota8t", [PART, PART])

    LNXp = sb("l0f_lnxp", [PART, NCHUNK * (S + 4)], F32)
    x1b = sb("l0f_x1b", [PART, NCHUNK * S], BF16)
    comb = sb("l0f_comb", [PART, NCHUNK * S], BF16)
    g_bf = sb("l0f_gbf", [PART, 2 * S], BF16)
    wbufA = sb("l0f_wbufA", [PART, NCHUNK * D], BF16)
    ew_bf = sb("l0f_ewbf", [PART, NCHUNK * D], BF16)

    rtw_f = sb("l0f_rtw", [PART, 64], F32)
    xbuf0 = sb("l0f_xbuf0", [PART, S], F32)
    xbuf = [xbuf0, xbuf0]
    sq_t = sb("l0f_sq", [PART, 512], F32)
    ye_t = sb("l0f_ye", [PART, 512], F32)
    t2_t = sb("l0f_t2", [PART, 512], BF16)
    prm_sb = sb("l0f_prm_sb", [PART, 24 + 8 + 8 + 64 + 8 + 8 + 2 + 8], F32)
    rb_sb = sb("l0f_rb_sb", [1, 8], F32)
    sel8 = sb("l0f_sel8", [8, D], BF16)
    idf = sb("l0f_idf", [PART, PART], F32)
    logits = sb("l0f_logits", [PART, PART], F32)
    tsel = sb("l0f_tsel", [PART, PART], F32)
    iota8t = sb("l0f_iota8t", [PART, PART], F32)
    m1 = sb("l0f_m1", [PART, 16], F32)
    m2 = sb("l0f_m2", [PART, 16], F32)
    found = sb("l0f_found", [PART, 16], F32)
    tww = sb("l0f_tww", [PART, 16], F32)
    w1 = sb("l0f_w1", [PART, 16], F32)
    w2 = sb("l0f_w2", [PART, 16], F32)
    selp = sb("l0f_selp", [PART, 128], F32)
    sel2p = sb("l0f_sel2p", [PART, 128], F32)
    l2p = sb("l0f_l2p", [PART, 128], F32)
    rw_t = sb("l0f_rw", [PART, 128], F32)
    rw8 = sb("l0f_rw8", [8, S], BF16)
    row_b = sb("l0f_rowb", [1, 512], F32)
    row_a = sb("l0f_rowa", [1, 512], F32)
    row_r = sb("l0f_rowr", [1, 512], F32)
    eps6_col = sb("l0f_eps6", [PART, 1], F32)
    tmpP = sb("l0f_tmpP", [PART, 512], F32)
    ffd_bf = ew_bf

    C_dwk, C_dwb, C_pwb, C_ewb, C_owb, C_nw, C_ffdb, C_ffub = 0, 24, 32, 40, 104, 112, 120, 122

    psA = ps("l0f_psA", [PART, 2048], F32)
    psC = ps("l0f_psC", [PART, 512], F32)
    psD = ps("l0f_psD", [PART, 512], F32)
    psE = ps("l0f_psE", [PART, 1024], F32)

    s_x = p.sem("l0f_x")
    s_w = p.sem("l0f_w")
    s_mm = p.sem("l0f_mm")
    s_ev = p.sem("l0f_ev")
    s_ve = p.sem("l0f_ve")
    s_ac = p.sem("l0f_ac")
    s_out = p.sem("l0f_out")

    p.vector.memset(eps6_col[:], 1e-6)
    p.sync.dma_start(out=prm_sb[:], in_=prm).then_inc(s_w, 16)
    p.sync.dma_start(out=rb_sb[:], in_=rb_h).then_inc(s_w, 16)
    p.sync.dma_start(out=sel8[:], in_=sel8_h).then_inc(s_w, 16)
    p.sync.dma_start(out=idf[:], in_=idf_h).then_inc(s_w, 16)
    p.sync.dma_start(out=iota8t[:], in_=iota_h).then_inc(s_w, 16)
    for c in range(NCHUNK):
        p.sync.dma_start(out=rtw_f[:, c * 8:(c + 1) * 8],
                         in_=rwT_h[c * PART:(c + 1) * PART, :]).then_inc(s_w, 16)
    prm_ready = s_w.n

    def col(base, idx):
        return prm_sb[:, base + idx: base + idx + 1]

    for b in range(BPC):
        x_hbm = x_hbm_list[b]
        y_hbm = y_hbm_list[b]

        mean_slots = [psA[0:1, t * 512:(t + 1) * 512] for t in range(4)]
        sq_slots = [psC[0:1, 0:512], psD[0:1, 0:512], psE[0:1, 0:512], psE[0:1, 512:1024]]
        p.tensor.wait(s_ev)
        p.tensor.wait(s_ve)
        p.tensor.wait(s_ac)
        xld = []
        for c in range(NCHUNK):
            buf = xbuf[c % 2]
            if c >= 1:
                p.sync.wait(s_ac, xld[c - 1])
                p.sync.wait(s_mm)
            p.sync.wait(s_out)
            p.sync.dma_start(out=buf[:], in_=x_hbm[c * PART:(c + 1) * PART, :]).then_inc(s_x, 16)
            p.tensor.wait(s_x)
            for t in range(4):
                p.tensor.matmul(mean_slots[t], kb.ones_col[:], buf[:, t * 512:(t + 1) * 512],
                                start=(c == 0), stop=(c == NCHUNK - 1))
            p.scalar.wait(s_x)
            for t in range(4):
                p.scalar.wait(s_mm)
                p.scalar.activation(out=sq_t[:], in_=buf[:, t * 512:(t + 1) * 512],
                                    func=AF.Square).then_inc(s_ac, 1)
                p.tensor.wait(s_ac)
                p.tensor.matmul(sq_slots[t], kb.ones_col[:], sq_t[:],
                                start=(c == 0), stop=(c == NCHUNK - 1)).then_inc(s_mm, 1)
            xld.append(s_ac.n)
        combf = comb[:].bitcast(F32)
        rstd_row = combf[0:1, 0:S]
        nmr_row = combf[0:1, S:2 * S]
        ln_marks = []
        for t in range(4):
            p.scalar.wait(s_mm)
            p.scalar.wait(s_ve)
            p.scalar.activation(out=row_a[0:1, :], in_=mean_slots[t], func=AF.Copy, scale=1.0 / D)
            p.scalar.activation(out=row_r[0:1, :], in_=row_a[0:1, :], func=AF.Square).then_inc(s_ac, 1)
            p.vector.wait(s_ac)
            p.vector.tensor_scalar(out=row_b[0:1, :], in0=sq_slots[t], scalar1=1.0 / D, scalar2=None,
                                   op0=ALU.mult)
            p.vector.tensor_tensor(out=row_b[0:1, :], in0=row_b[0:1, :], in1=row_r[0:1, :],
                                   op=ALU.subtract).then_inc(s_ve, 1)
            p.scalar.wait(s_ve)
            p.scalar.activation(out=row_r[0:1, :], in_=row_b[0:1, :], func=AF.Ln, bias=kb.eps_col[0:1, :])
            p.scalar.activation(out=rstd_row[0:1, t * 512:(t + 1) * 512], in_=row_r[0:1, :],
                                func=AF.Exp, scale=-0.5).then_inc(s_ac, 1)
            p.vector.wait(s_ac)
            p.vector.scalar_tensor_tensor(out=nmr_row[0:1, t * 512:(t + 1) * 512],
                                          in0=row_a[0:1, :], scalar=-1.0,
                                          in1=rstd_row[0:1, t * 512:(t + 1) * 512],
                                          op0=ALU.mult, op1=ALU.mult).then_inc(s_ve, 1)
        rows_done = s_ve.n
        p.barrier(f"l0f{b}_rows")
        if getattr(kb, "debug", False) and b == 0:
            hbr = kb.dram_out("d_rows", [1, 2 * S], F32)
            p.sync.wait(s_ve, rows_done)
            p.sync.dma_start(out=hbr, in_=combf[0:1, 0:2 * S]).then_inc(s_out, 16)
        p.tensor.wait(s_ve, rows_done)
        p.tensor.wait(s_ac)
        xld2 = []
        for t in range(4):
            p.tensor.wait(s_ev)
            p.tensor.matmul(psC[:, 0:512], kb.ones_row[:], rstd_row[0:1, t * 512:(t + 1) * 512],
                            start=True, stop=True)
            p.tensor.matmul(psD[:, 0:512], kb.ones_row[:], nmr_row[0:1, t * 512:(t + 1) * 512],
                            start=True, stop=True).then_inc(s_mm, 1)
            for c in range(NCHUNK):
                p.sync.wait(s_ve)
                p.sync.dma_start(out=xbuf0[:, 0:512],
                                 in_=x_hbm[c * PART:(c + 1) * PART, t * 512:(t + 1) * 512]).then_inc(s_x, 16)
                p.vector.wait(s_x)
                p.vector.wait(s_mm)
                base = c * (S + 4) + 4
                p.vector.tensor_tensor(out=tmpP[:, 0:512], in0=xbuf0[:, 0:512],
                                       in1=psC[:, 0:512], op=ALU.mult)
                p.vector.tensor_tensor(out=LNXp[:, base + t * 512: base + (t + 1) * 512],
                                       in0=tmpP[:, 0:512], in1=psD[:, 0:512], op=ALU.add).then_inc(s_ve, 1)
            p.vector.last().then_inc(s_ev, 1)
        for c in range(NCHUNK):
            p.vector.memset(LNXp[:, c * (S + 4):c * (S + 4) + 4], 0.0)
        p.vector.last().then_inc(s_ve, 1)
        ln_apply_done = s_ve.n
        p.barrier(f"l0f{b}_ln")
        p.tensor.wait(s_ve, ln_apply_done)

        p.gpsimd.wait(s_mm)
        for c in range(NCHUNK):
            p.gpsimd.dma_start(out=wbufA[:, c * D:(c + 1) * D],
                               in_=wpwT[c * PART:(c + 1) * PART, :]).then_inc(s_w, 16)
        pw_ready = s_w.n
        p.vector.wait(s_w, prm_ready)
        for c in range(NCHUNK):
            base = c * (S + 4)
            for t in range(4):
                o = base + t * 512 + 2
                p.vector.tensor_scalar(out=tmpP[:, 0:512], in0=LNXp[:, o:o + 512],
                                       scalar1=col(C_dwk, 3 * c + 0), scalar2=col(C_dwb, c),
                                       op0=ALU.mult, op1=ALU.add)
                p.vector.scalar_tensor_tensor(out=tmpP[:, 0:512], in0=LNXp[:, o + 1:o + 1 + 512],
                                              scalar=col(C_dwk, 3 * c + 1), in1=tmpP[:, 0:512],
                                              op0=ALU.mult, op1=ALU.add)
                p.vector.scalar_tensor_tensor(out=comb[:, c * S + t * 512: c * S + (t + 1) * 512],
                                              in0=LNXp[:, o + 2:o + 2 + 512],
                                              scalar=col(C_dwk, 3 * c + 2), in1=tmpP[:, 0:512],
                                              op0=ALU.mult, op1=ALU.add).then_inc(s_ve, 1)
        conv_done = s_ve.n
        p.barrier(f"l0f{b}_conv")
        p.tensor.wait(s_ve, conv_done)
        p.tensor.wait(s_w, pw_ready)
        for oc in range(NCHUNK):
            for t in range(4):
                p.tensor.wait(s_ac)
                for ic in range(NCHUNK):
                    p.tensor.matmul(psA[:, t * 512:(t + 1) * 512],
                                    wbufA[:, ic * D + oc * PART: ic * D + (oc + 1) * PART],
                                    comb[:, ic * S + t * 512: ic * S + (t + 1) * 512],
                                    start=(ic == 0), stop=(ic == NCHUNK - 1))
                p.tensor.last().then_inc(s_mm, 1)
                p.scalar.wait(s_mm)
                p.scalar.activation(out=ye_t[:], in_=psA[:, t * 512:(t + 1) * 512],
                                    func=AF.Silu, bias=col(C_pwb, oc)).then_inc(s_ac, 1)
                p.vector.wait(s_ac)
                base = oc * (S + 4) + 4 + t * 512
                p.vector.tensor_tensor(out=LNXp[:, base:base + 512], in0=LNXp[:, base:base + 512],
                                       in1=ye_t[:], op=ALU.add)
                p.vector.tensor_copy(out=x1b[:, oc * S + t * 512: oc * S + (t + 1) * 512],
                                     in_=LNXp[:, base:base + 512]).then_inc(s_ve, 1)
                p.scalar.wait(s_ve)
        x1_done = s_ve.n
        p.barrier(f"l0f{b}_x1")
        if getattr(kb, "debug", False) and b == 0:
            hb = kb.dram_out("d_x1", [PART, NCHUNK * (S + 4)], F32)
            p.sync.wait(s_ve, x1_done)
            p.sync.dma_start(out=hb, in_=LNXp[:]).then_inc(s_out, 16)
            hb2 = kb.dram_out("d_conv", [PART, NCHUNK * S], BF16)
            p.sync.dma_start(out=hb2, in_=comb[:]).then_inc(s_out, 16)

        for ti in range(16):
            p.tensor.wait(s_ve, x1_done)
            p.tensor.wait(s_w)
            for c in range(NCHUNK):
                p.tensor.matmul(psD[:, (ti % 2) * 256 + 0:(ti % 2) * 256 + 8],
                                LNXp[:, c * (S + 4) + 4 + ti * PART: c * (S + 4) + 4 + (ti + 1) * PART],
                                rtw_f[:, c * 8:(c + 1) * 8],
                                start=(c == 0), stop=(c == NCHUNK - 1))
            p.tensor.last().then_inc(s_mm, 1)
            p.vector.wait(s_mm)
            p.vector.tensor_copy(out=logits[:, ti * 8:(ti + 1) * 8],
                                 in_=psD[:, (ti % 2) * 256 + 0:(ti % 2) * 256 + 8]).then_inc(s_ev, 1)
            p.tensor.wait(s_ev)
        p.tensor.matmul(psC[:, 0:8], kb.ones_row[:], rb_sb[0:1, :], start=True, stop=True).then_inc(s_mm, 1)
        p.vector.wait(s_mm)
        p.vector.wait(s_w, prm_ready)
        L3 = logits[:].rearrange("p (a e) -> p a e", e=8)
        bc8 = psC[:, 0:8].rearrange("p (a e) -> p a e", a=1).broadcast_to([PART, 16, 8])
        p.vector.tensor_tensor(out=L3, in0=L3, in1=bc8, op=ALU.add)
        iota_b = iota8t[:].rearrange("p (a e) -> p a e", e=8)

        def first_match(src3, selT3):
            p.vector.reduce_max(out=found[:], in_=src3, axis=AX.X)
            p.vector.fence(s_ve)
            m1b_ = found[:].rearrange("p (a e) -> p a e", e=1).broadcast_to([PART, 16, 8])
            T3 = tsel[:].rearrange("p (a e) -> p a e", e=8)
            p.vector.tensor_tensor(out=T3, in0=src3, in1=m1b_, op=ALU.is_ge)
            p.vector.fence(s_ve)
            p.vector.tensor_scalar(out=tsel[:], in0=tsel[:], scalar1=8.0, scalar2=None, op0=ALU.mult)
            p.vector.fence(s_ve)
            p.vector.tensor_tensor(out=tsel[:], in0=tsel[:], in1=iota8t[:], op=ALU.subtract)
            p.vector.fence(s_ve)
            p.vector.reduce_max(out=tww[:], in_=tsel[:].rearrange("p (a e) -> p a e", e=8), axis=AX.X)
            p.vector.fence(s_ve)
            twwb = tww[:].rearrange("p (a e) -> p a e", e=1).broadcast_to([PART, 16, 8])
            p.vector.tensor_tensor(out=selT3, in0=tsel[:].rearrange("p (a e) -> p a e", e=8),
                                   in1=twwb, op=ALU.is_ge)
            p.vector.fence(s_ve)

        first_match(L3, selp[:].rearrange("p (a e) -> p a e", e=8))
        S3 = selp[:].rearrange("p (a e) -> p a e", e=8)
        p.vector.reduce_max(out=m1[:], in_=L3, axis=AX.X)
        p.vector.fence(s_ve)
        p.vector.scalar_tensor_tensor(out=l2p[:], in0=selp[:], scalar=-1e30, in1=logits[:],
                                      op0=ALU.mult, op1=ALU.add)
        p.vector.fence(s_ve)
        L2 = l2p[:].rearrange("p (a e) -> p a e", e=8)
        first_match(L2, sel2p[:].rearrange("p (a e) -> p a e", e=8))
        S23 = sel2p[:].rearrange("p (a e) -> p a e", e=8)
        p.vector.reduce_max(out=m2[:], in_=L2, axis=AX.X)
        p.vector.fence(s_ve)
        p.vector.tensor_tensor(out=tww[:], in0=m2[:], in1=m1[:], op=ALU.subtract).then_inc(s_ve, 1)
        p.scalar.wait(s_ve)
        p.scalar.activation(out=m2[:], in_=tww[:], func=AF.Exp).then_inc(s_ac, 1)
        p.vector.wait(s_ac)
        p.vector.tensor_scalar(out=w1[:], in0=m2[:], scalar1=1.0, scalar2=None, op0=ALU.add)
        p.vector.fence(s_ve)
        p.vector.reciprocal(out=w1[:], in_=w1[:])
        p.vector.fence(s_ve)
        p.vector.tensor_tensor(out=w2[:], in0=m2[:], in1=w1[:], op=ALU.mult)
        p.vector.fence(s_ve)
        w1b = w1[:].rearrange("p (a e) -> p a e", e=1).broadcast_to([PART, 16, 8])
        w2b = w2[:].rearrange("p (a e) -> p a e", e=1).broadcast_to([PART, 16, 8])
        RW3 = rw_t[:].rearrange("p (a e) -> p a e", e=8)
        p.vector.tensor_tensor(out=RW3, in0=S3, in1=w1b, op=ALU.mult)
        p.vector.fence(s_ve)
        p.vector.tensor_tensor(out=l2p[:].rearrange("p (a e) -> p a e", e=8), in0=S23, in1=w2b, op=ALU.mult)
        p.vector.fence(s_ve)
        p.vector.tensor_tensor(out=rw_t[:], in0=rw_t[:], in1=l2p[:], op=ALU.add).then_inc(s_ve, 1)
        p.tensor.wait(s_ve)
        p.tensor.wait(s_ev)
        for ti in range(16):
            p.tensor.transpose(psA[0:8, ti * PART:(ti + 1) * PART],
                               rw_t[:, ti * 8:(ti + 1) * 8], idf[:])
        p.tensor.last().then_inc(s_mm, 1)
        p.vector.wait(s_mm)
        p.vector.tensor_copy(out=rw8[:], in_=psA[0:8, 0:S]).then_inc(s_ev, 1)
        p.tensor.wait(s_ev)
        if getattr(kb, "debug", False) and b == 0:
            for nm2, t2 in (("d_selp", selp), ("d_sel2p", sel2p), ("d_rwt", rw_t),
                            ("d_tsel", tsel), ("d_l2p", l2p)):
                hbx = kb.dram_out(nm2, [PART, PART], F32)
                p.sync.wait(s_ev)
                p.sync.dma_start(out=hbx, in_=t2[:]).then_inc(s_out, 16)
            hbw = kb.dram_out("d_w12", [PART, 32], F32)
            p.sync.dma_start(out=hbw[0:PART, 0:16], in_=w1[:]).then_inc(s_out, 16)
            p.sync.dma_start(out=hbw[0:PART, 16:32], in_=w2[:]).then_inc(s_out, 16)
            hb3 = kb.dram_out("d_logits", [PART, PART], F32)
            p.sync.wait(s_ev)
            p.sync.dma_start(out=hb3, in_=logits[:]).then_inc(s_out, 16)
            hb4 = kb.dram_out("d_rw8", [8, S], BF16)
            p.sync.dma_start(out=hb4, in_=rw8[:]).then_inc(s_out, 16)

        p.barrier(f"l0f{b}_rt")
        ew_marks = [0]
        for e in range(8):
            if e >= 1:
                p.gpsimd.wait(s_mm, ew_marks[-1])
            else:
                p.gpsimd.wait(s_mm)
            for ic in range(NCHUNK):
                p.gpsimd.dma_start(out=ew_bf[:, ic * D:(ic + 1) * D],
                                   in_=wewT[e, ic * PART:(ic + 1) * PART, :]).then_inc(s_w, 16)
            ew_ready = s_w.n
            p.tensor.wait(s_w, ew_ready)
            for t in range(4):
                p.tensor.wait(s_ve)
                p.tensor.matmul(psC[:, 0:512], sel8[:, e * PART:(e + 1) * PART],
                                rw8[:, t * 512:(t + 1) * 512], start=True, stop=True).then_inc(s_mm, 1)
                for oc in range(NCHUNK):
                    slot = (oc % 4) * 512
                    p.tensor.wait(s_ac)
                    for ic in range(NCHUNK):
                        p.tensor.matmul(psA[:, slot:slot + 512],
                                        ew_bf[:, ic * D + oc * PART: ic * D + (oc + 1) * PART],
                                        x1b[:, ic * S + t * 512: ic * S + (t + 1) * 512],
                                        start=(ic == 0), stop=(ic == NCHUNK - 1))
                    p.tensor.last().then_inc(s_mm, 1)
                    p.scalar.wait(s_mm)
                    p.scalar.activation(out=ye_t[:], in_=psA[:, slot:slot + 512],
                                        func=AF.Silu, bias=col(C_ewb, e * 8 + oc)).then_inc(s_ac, 1)
                    p.vector.wait(s_ac)
                    dst = comb[:, oc * S + t * 512: oc * S + (t + 1) * 512]
                    if e == 0:
                        p.vector.tensor_tensor(out=dst, in0=ye_t[:], in1=psC[:, 0:512],
                                               op=ALU.mult).then_inc(s_ve, 1)
                    else:
                        p.vector.tensor_tensor(out=t2_t[:], in0=ye_t[:], in1=psC[:, 0:512],
                                               op=ALU.mult)
                        p.vector.tensor_tensor(out=dst, in0=dst, in1=t2_t[:], op=ALU.add).then_inc(s_ve, 1)
                    p.scalar.wait(s_ve)
            ew_marks.append(s_mm.n)

        if getattr(kb, "debug", False) and b == 0:
            hb5 = kb.dram_out("d_comb", [PART, NCHUNK * S], BF16)
            p.sync.wait(s_ve)
            p.sync.dma_start(out=hb5, in_=comb[:]).then_inc(s_out, 16)
        p.barrier(f"l0f{b}_ex")
        for c in range(NCHUNK):
            p.gpsimd.wait(s_mm)
            p.gpsimd.dma_start(out=wbufA[:, c * D:(c + 1) * D],
                               in_=wowT[c * PART:(c + 1) * PART, :]).then_inc(s_w, 16)
        ow_ready = s_w.n
        p.tensor.wait(s_w, ow_ready)
        p.tensor.wait(s_ve)
        for oc in range(NCHUNK):
            for t in range(4):
                p.tensor.wait(s_ev)
                for ic in range(NCHUNK):
                    p.tensor.matmul(psA[:, t * 512:(t + 1) * 512],
                                    wbufA[:, ic * D + oc * PART: ic * D + (oc + 1) * PART],
                                    comb[:, ic * S + t * 512: ic * S + (t + 1) * 512],
                                    start=(ic == 0), stop=(ic == NCHUNK - 1))
                p.tensor.last().then_inc(s_mm, 1)
                p.vector.wait(s_mm)
                base = oc * (S + 4) + 4 + t * 512
                p.vector.tensor_scalar(out=tmpP[:, 0:512], in0=psA[:, t * 512:(t + 1) * 512],
                                       scalar1=col(C_owb, oc), scalar2=None, op0=ALU.add)
                p.vector.tensor_tensor(out=LNXp[:, base:base + 512], in0=LNXp[:, base:base + 512],
                                       in1=tmpP[:, 0:512], op=ALU.add).then_inc(s_ev, 1)
        z_done = s_ev.n

        p.barrier(f"l0f{b}_z")
        for t in range(4):
            sp0 = 4 + t * 512
            p.scalar.wait(s_ev, z_done)
            for c in range(NCHUNK):
                p.scalar.activation(out=sq_t[:], in_=LNXp[:, c * (S + 4) + sp0: c * (S + 4) + sp0 + 512],
                                    func=AF.Square).then_inc(s_ac, 1)
                p.tensor.wait(s_ac)
                if c == 0:
                    p.tensor.wait(s_ev)
                p.tensor.matmul(psD[0:1, 0:512], kb.ones_col[:], sq_t[:],
                                start=(c == 0), stop=(c == NCHUNK - 1))
                p.tensor.last().then_inc(s_mm, 1)
                p.scalar.wait(s_mm)
            p.scalar.activation(out=row_r[0:1, :], in_=psD[0:1, 0:512], func=AF.Ln,
                                scale=1.0 / D, bias=eps6_col[0:1, :])
            p.scalar.activation(out=row_a[0:1, :], in_=row_r[0:1, :], func=AF.Exp, scale=-0.5).then_inc(s_ac, 1)
            p.tensor.wait(s_ac)
            p.tensor.matmul(psC[:, 0:512], kb.ones_row[:], row_a[0:1, :], start=True, stop=True).then_inc(s_mm, 1)
            p.vector.wait(s_mm)
            for c in range(NCHUNK):
                base = c * (S + 4) + sp0
                p.vector.tensor_tensor(out=LNXp[:, base:base + 512], in0=LNXp[:, base:base + 512],
                                       in1=psC[:, 0:512], op=ALU.mult)
                p.vector.tensor_scalar(out=LNXp[:, base:base + 512], in0=LNXp[:, base:base + 512],
                                       scalar1=col(C_nw, c), scalar2=None, op0=ALU.mult)
                p.vector.tensor_copy(out=x1b[:, c * S + t * 512: c * S + (t + 1) * 512],
                                     in_=LNXp[:, base:base + 512])
            p.vector.last().then_inc(s_ve, 1)
            p.tensor.wait(s_ve)
        h_done = s_ve.n
        if getattr(kb, "debug", False) and b == 0:
            hb6 = kb.dram_out("d_h", [PART, NCHUNK * (S + 4)], F32)
            p.sync.wait(s_ve, h_done)
            p.sync.dma_start(out=hb6, in_=LNXp[:]).then_inc(s_out, 16)

        p.barrier(f"l0f{b}_h")
        p.gpsimd.wait(s_mm)
        for c in range(NCHUNK):
            p.gpsimd.dma_start(out=ffd_bf[:, c * 256:(c + 1) * 256],
                               in_=wffdT[c * PART:(c + 1) * PART, :]).then_inc(s_w, 16)
        for r2 in range(2):
            p.gpsimd.dma_start(out=wbufA[:, r2 * D:(r2 + 1) * D],
                               in_=wffuT[r2 * PART:(r2 + 1) * PART, :]).then_inc(s_w, 16)
        ffw_ready = s_w.n
        p.tensor.wait(s_w, ffw_ready)
        p.tensor.wait(s_ve, h_done)
        if getattr(kb, "debug", False) and b == 0:
            hbw2 = kb.dram_out("d_ffdw", [PART, 2048], BF16)
            p.sync.wait(s_w, ffw_ready)
            p.sync.dma_start(out=hbw2, in_=ffd_bf[:, 0:2048]).then_inc(s_out, 16)
            hbw3 = kb.dram_out("d_ffuw", [PART, 2048], BF16)
            p.sync.dma_start(out=hbw3, in_=wbufA[:, 0:2048]).then_inc(s_out, 16)
        for hc in range(2):
            for t in range(4):
                p.tensor.wait(s_ac)
                for ic in range(NCHUNK):
                    p.tensor.matmul(psA[:, t * 512:(t + 1) * 512],
                                    ffd_bf[:, ic * 256 + hc * PART: ic * 256 + (hc + 1) * PART],
                                    x1b[:, ic * S + t * 512: ic * S + (t + 1) * 512],
                                    start=(ic == 0), stop=(ic == NCHUNK - 1))
                p.tensor.last().then_inc(s_mm, 1)
                p.scalar.wait(s_mm)
                p.scalar.activation(out=g_bf[:, hc * S + t * 512: hc * S + (t + 1) * 512],
                                    in_=psA[:, t * 512:(t + 1) * 512],
                                    func=AF.Gelu, bias=col(C_ffdb, hc)).then_inc(s_ac, 1)
        g_done = s_ac.n
        if getattr(kb, "debug", False) and b == 0:
            hbg = kb.dram_out("d_g", [PART, 2 * S], BF16)
            p.sync.wait(s_ac, g_done)
            p.sync.dma_start(out=hbg, in_=g_bf[:]).then_inc(s_out, 16)
        p.tensor.wait(s_ac, g_done)
        xr_marks = []
        for oc in range(NCHUNK):
            bufx = xbuf0
            if oc >= 1:
                p.sync.wait(s_out, xr_marks[oc - 1])
            p.sync.dma_start(out=bufx[:], in_=x_hbm[oc * PART:(oc + 1) * PART, :]).then_inc(s_x, 16)
            for t in range(4):
                p.tensor.wait(s_ev)
                for hc in range(2):
                    p.tensor.matmul(psA[:, t * 512:(t + 1) * 512],
                                    wbufA[:, hc * D + oc * PART: hc * D + (oc + 1) * PART],
                                    g_bf[:, hc * S + t * 512: hc * S + (t + 1) * 512],
                                    start=(hc == 0), stop=(hc == 1))
                p.tensor.last().then_inc(s_mm, 1)
                p.vector.wait(s_mm)
                p.vector.wait(s_x)
                base = oc * (S + 4) + 4 + t * 512
                p.vector.tensor_scalar(out=tmpP[:, 0:512], in0=psA[:, t * 512:(t + 1) * 512],
                                       scalar1=col(C_ffub, oc), scalar2=None, op0=ALU.add)
                p.vector.tensor_tensor(out=tmpP[:, 0:512], in0=tmpP[:, 0:512],
                                       in1=LNXp[:, base:base + 512], op=ALU.add)
                p.vector.tensor_tensor(out=bufx[:, t * 512:(t + 1) * 512], in0=bufx[:, t * 512:(t + 1) * 512],
                                       in1=tmpP[:, 0:512], op=ALU.add).then_inc(s_ev, 1)
            p.sync.wait(s_ev)
            p.sync.dma_start(out=y_hbm[oc * PART:(oc + 1) * PART, :], in_=bufx[:]).then_inc(s_out, 16)
            xr_marks.append(s_out.n)
        p.barrier(f"l0f{b}_end")
    p.sync.wait(s_out)




def build_l1_attn(kb, x_hbm_list, y_hbm_list):
    nc, p = kb.nc, kb.p
    sb, ps = kb.sbuf, kb.psum
    QT = 256
    NQ = S // QT
    SS = 256
    NS = S // SS

    aw_h = kb.dram_in("L1a_awT", [4, D, 3 * D])
    aproj_h = kb.dram_in("L1a_apT", [4, D, D])
    fc_h = kb.dram_in("L1a_fcT", [4, D, 4 * D])
    mp_h = kb.dram_in("L1a_mpT", [4, 4 * D, D])
    prm_h = kb.dram_in("L1a_prm", [PART, 4 * 104])
    A_h = kb.dram_in("c_Abf", [PART, 64 * PART], BF16)
    B_h = kb.dram_in("c_Bbf", [PART, 64 * PART], BF16)
    fc_filt_h = kb.dram_in("c_filtc", [D, S])
    fs_filt_h = kb.dram_in("c_filts", [D, S])
    xl_hbm = [kb.dram_tmp(f"l1a_xl{b}", [D, S]) for b in range(BPC)]
    xl2_hbm = [kb.dram_tmp(f"l1a_xl2{b}", [D, S]) for b in range(BPC)]

    prm_sb = sb("l1a_prm_sb", [PART, 4 * 104], F32)
    row_a = sb("l1a_rowa", [1, QT + 1], F32)
    row_b = sb("l1a_rowb", [1, QT + 1], F32)
    row_c = sb("l1a_rowc", [1, QT + 1], F32)

    def pcol(i, base, idx):
        o = i * 104 + base + idx
        return prm_sb[:, o:o + 1]
    B_ln1w, B_ln1b, B_ln2w, B_ln2b, B_ab, B_apb, B_fcb, B_mpb = 0, 8, 16, 24, 32, 56, 64, 96

    AR = sb("l1a_arena", [PART, 200 * 1024], mybir.dt.uint8)

    def carve(off_kb, n_el, dtype):
        esz = 2 if dtype in (BF16,) else 4
        v = AR[:, off_kb * 1024:off_kb * 1024 + n_el * esz].bitcast(dtype)
        return v

    A_bf = carve(0, 8192, BF16)
    B_bf = carve(16, 8192, BF16)
    aw_bf = carve(32, 24576, BF16)
    ap_bf = carve(80, 8192, BF16)
    qkvt = [carve(96 + 5 * t, 8 * (QT + 1), BF16) for t in range(3)]
    lnx_q = carve(112, 8 * (QT + 1), BF16)
    a_bfp = carve(117, 8 * QT, BF16)
    cmp_q = [carve(122 + 4 * i, 8 * QT, BF16) for i in range(3)]
    cmp_k = [carve(134 + 4 * i, 8 * QT, BF16) for i in range(3)]
    v_l = carve(146, 8 * QT, F32)
    qk_l = carve(154, 8 * QT, F32)
    tmp_l = carve(162, 8 * QT, F32)
    p0_t = carve(170, 8 * QT, F32)
    x0_bf = carve(178, 8 * QT, BF16)
    zr_t = carve(182, 8 * QT, F32)
    zi_t = carve(190, 8 * QT, F32)
    fc_t = carve(122, 8 * QT, F32)
    fs_t = carve(130, 8 * QT, F32)
    wr_bf = carve(138, 8 * QT, BF16)
    wi_bf = carve(142, 8 * QT, BF16)
    xl_q = carve(146, 8 * (QT + 1), F32)

    fc_bf = carve(0, 8 * 4096, BF16)
    mp_bf = carve(64, 32 * 1024, BF16)
    lnx2 = carve(128, 8 * SS, BF16)
    g_bf2 = [carve(136 + 2 * i, SS, BF16) for i in range(2)]
    xl_s = carve(140, 8 * SS, F32)
    tmp_s = carve(156, SS, F32)

    psA = ps("l1a_psA", [PART, 2048], F32)
    psC = ps("l1a_psC", [PART, 512], F32)
    psD = ps("l1a_psD", [PART, 512], F32)
    psE = ps("l1a_psE", [PART, 1024], F32)

    s_x = p.sem("l1a_x")
    s_w = p.sem("l1a_w")
    s_mm = p.sem("l1a_mm")
    s_ev = p.sem("l1a_ev")
    s_ve = p.sem("l1a_ve")
    s_ac = p.sem("l1a_ac")
    s_gp = p.sem("l1a_gp")
    s_out = p.sem("l1a_out")

    p.sync.dma_start(out=prm_sb[:], in_=prm_h).then_inc(s_w, 16)

    def seg(tile, c, width, off=0):
        return tile[:, c * width + off: (c + 1) * width]

    def ln_quarter(src_loader, n_tok, out_bf, wcol, bcol, out_f32=None):
        p.tensor.wait(s_x)
        p.tensor.wait(s_ac)
        p.tensor.wait(s_ve)
        p.tensor.wait(s_ev)
        for c in range(NCHUNK):
            p.tensor.matmul(psD[0:1, 0:n_tok], kb.ones_col[:], src_loader(c),
                            start=(c == 0), stop=(c == NCHUNK - 1))
        p.tensor.last().then_inc(s_mm, 1)
        for c in range(NCHUNK):
            p.scalar.wait(s_mm)
            p.scalar.activation(out=tmp_l[:, 0:n_tok], in_=src_loader(c), func=AF.Square).then_inc(s_ac, 1)
            p.tensor.wait(s_ac)
            p.tensor.matmul(psC[0:1, 0:n_tok], kb.ones_col[:], tmp_l[:, 0:n_tok],
                            start=(c == 0), stop=(c == NCHUNK - 1)).then_inc(s_mm, 1)
        p.scalar.wait(s_mm)
        p.scalar.wait(s_ve)
        p.scalar.activation(out=row_a[0:1, 0:n_tok], in_=psD[0:1, 0:n_tok], func=AF.Copy, scale=1.0 / D)
        p.scalar.activation(out=row_b[0:1, 0:n_tok], in_=row_a[0:1, 0:n_tok], func=AF.Square).then_inc(s_ac, 1)
        p.vector.wait(s_ac)
        p.vector.tensor_scalar(out=row_c[0:1, 0:n_tok], in0=psC[0:1, 0:n_tok], scalar1=1.0 / D,
                               scalar2=None, op0=ALU.mult)
        p.vector.tensor_tensor(out=row_c[0:1, 0:n_tok], in0=row_c[0:1, 0:n_tok],
                               in1=row_b[0:1, 0:n_tok], op=ALU.subtract).then_inc(s_ve, 1)
        p.scalar.wait(s_ve)
        p.scalar.activation(out=row_b[0:1, 0:n_tok], in_=row_c[0:1, 0:n_tok], func=AF.Ln,
                            bias=kb.eps_col[0:1, :])
        p.scalar.activation(out=row_c[0:1, 0:n_tok], in_=row_b[0:1, 0:n_tok], func=AF.Exp,
                            scale=-0.5).then_inc(s_ac, 1)
        p.vector.wait(s_ac)
        p.vector.scalar_tensor_tensor(out=row_b[0:1, 0:n_tok], in0=row_a[0:1, 0:n_tok], scalar=-1.0,
                                      in1=row_c[0:1, 0:n_tok], op0=ALU.mult, op1=ALU.mult).then_inc(s_ve, 1)
        p.tensor.wait(s_ve)
        p.tensor.matmul(psC[:, 0:n_tok], kb.ones_row[:], row_c[0:1, 0:n_tok], start=True, stop=True)
        p.tensor.matmul(psD[:, 0:n_tok], kb.ones_row[:], row_b[0:1, 0:n_tok],
                        start=True, stop=True).then_inc(s_mm, 1)
        p.vector.wait(s_mm)
        for c in range(NCHUNK):
            p.vector.tensor_tensor(out=tmp_l[:, 0:n_tok], in0=src_loader(c), in1=psC[:, 0:n_tok],
                                   op=ALU.mult)
            p.vector.tensor_tensor(out=tmp_l[:, 0:n_tok], in0=tmp_l[:, 0:n_tok], in1=psD[:, 0:n_tok],
                                   op=ALU.add)
            if wcol is not None:
                p.vector.tensor_scalar(out=out_bf(c), in0=tmp_l[:, 0:n_tok],
                                       scalar1=wcol(c), scalar2=bcol(c),
                                       op0=ALU.mult, op1=ALU.add)
            else:
                p.vector.tensor_copy(out=out_bf(c), in_=tmp_l[:, 0:n_tok])
            if out_f32 is not None:
                p.vector.tensor_copy(out=out_f32(c), in_=tmp_l[:, 0:n_tok])
        p.vector.last().then_inc(s_ve, 1)

    def cosops(dst_bf, src_view, n, scratch):
        p.vector.tensor_scalar(out=tmp_l[:, 0:n], in0=src_view, scalar1=1.0 / TWO_PI,
                               scalar2=0.25, op0=ALU.mult, op1=ALU.subtract)
        p.vector.tensor_copy(out=scratch[:, 0:n].bitcast(I32), in_=tmp_l[:, 0:n])
        p.vector.tensor_copy(out=scratch[:, 0:n], in_=scratch[:, 0:n].bitcast(I32))
        p.vector.tensor_tensor(out=tmp_l[:, 0:n], in0=tmp_l[:, 0:n], in1=scratch[:, 0:n],
                               op=ALU.subtract).then_inc(s_ve, 1)
        p.scalar.wait(s_ve)
        p.scalar.activation(out=dst_bf, in_=tmp_l[:, 0:n], func=AF.Sin, scale=-TWO_PI).then_inc(s_ac, 1)
        p.vector.wait(s_ac)

    for b in range(BPC):
        x_hbm = x_hbm_list[b]
        y_hbm = y_hbm_list[b]
        xlbufs = [xl_hbm[b], xl2_hbm[b]]
        cur = 0
        xl = xlbufs[0]

        for qtr in range(NQ):
            q0 = qtr * QT
            p.sync.wait(s_ve)
            p.sync.wait(s_out)
            for c in range(NCHUNK):
                p.sync.dma_start(out=seg(xl_q, c, QT), in_=x_hbm[c * PART:(c + 1) * PART, q0:q0 + QT]).then_inc(s_x, 16)
            ln_quarter(lambda c: seg(xl_q, c, QT), QT,
                       lambda c: seg(x0_bf, c, QT), None, None,
                       out_f32=lambda c: seg(p0_t, c, QT))
            p.sync.wait(s_ve)
            for c in range(NCHUNK):
                p.sync.dma_start(out=xl[c * PART:(c + 1) * PART, q0:q0 + QT], in_=seg(p0_t, c, QT)).then_inc(s_out, 16)
        p.barrier(f"l1a{b}_pre")
        if getattr(kb, "debug", False) and b == 0:
            hbp = kb.dram_out("d_xl0", [D, S], F32)
            p.sync.dma_start(out=hbp, in_=xl[:, :]).then_inc(s_out, 16)
            p.sync.wait(s_out)
            p.barrier("dbg_pre")

        for i in range(4):
            xl_src = xlbufs[cur]
            xl_dst = xlbufs[1 - cur]
            cur = 1 - cur
            p.gpsimd.wait(s_mm)
            p.gpsimd.dma_start(out=A_bf[:], in_=A_h).then_inc(s_w, 16)
            p.gpsimd.dma_start(out=B_bf[:], in_=B_h).then_inc(s_w, 16)
            for c in range(NCHUNK):
                p.gpsimd.dma_start(out=aw_bf[:, c * 3072:(c + 1) * 3072],
                                   in_=aw_h[i, c * PART:(c + 1) * PART, :]).then_inc(s_w, 16)
            for c in range(NCHUNK):
                p.gpsimd.dma_start(out=ap_bf[:, c * D:(c + 1) * D],
                                   in_=aproj_h[i, c * PART:(c + 1) * PART, :]).then_inc(s_w, 16)
            wA_ready = s_w.n
            for qtr in range(NQ):
                q0 = qtr * QT
                nt = QT + 1
                p.sync.wait(s_ve)
                p.sync.wait(s_mm)
                for c in range(NCHUNK):
                    if qtr == 0:
                        p.sync.dma_start(out=seg(xl_q, c, nt, 0)[:, 0:1],
                                         in_=xl_src[c * PART:(c + 1) * PART, 0:1],
                                         allow_slow_non_contiguous=True).then_inc(s_x, 16)
                        p.sync.dma_start(out=seg(xl_q, c, nt)[:, 1:nt],
                                         in_=xl_src[c * PART:(c + 1) * PART, 0:QT]).then_inc(s_x, 16)
                    else:
                        p.sync.dma_start(out=seg(xl_q, c, nt),
                                         in_=xl_src[c * PART:(c + 1) * PART, q0 - 1:q0 + QT]).then_inc(s_x, 16)
                ln_quarter(lambda c: seg(xl_q, c, nt), nt,
                           lambda c: seg(lnx_q, c, nt),
                           lambda c: pcol(i, B_ln1w, c), lambda c: pcol(i, B_ln1b, c))
                p.tensor.wait(s_w, wA_ready)
                p.tensor.wait(s_ve)
                for t in range(3):
                    for oc in range(NCHUNK):
                        p.tensor.wait(s_ev)
                        for c in range(NCHUNK):
                            lhsT = aw_bf[:, c * 3072 + t * D + oc * PART: c * 3072 + t * D + (oc + 1) * PART]
                            p.tensor.matmul(psA[:, (oc % 4) * 512:(oc % 4) * 512 + nt], lhsT,
                                            seg(lnx_q, c, nt),
                                            start=(c == 0), stop=(c == NCHUNK - 1))
                        p.tensor.last().then_inc(s_mm, 1)
                        p.scalar.wait(s_mm)
                        p.scalar.activation(out=seg(qkvt[t], oc, nt), in_=psA[:, (oc % 4) * 512:(oc % 4) * 512 + nt],
                                            func=AF.Identity, bias=pcol(i, B_ab, t * 8 + oc)).then_inc(s_ev, 1)
                p.barrier(f"l1a{b}_{i}_{qtr}_qkv")
                if getattr(kb, "debug", False) and b == 0 and i == 0 and qtr == 1:
                    hq1 = kb.dram_out("d_q1_lnx", [PART, 8 * (QT + 1)], BF16)
                    p.sync.dma_start(out=hq1, in_=lnx_q[:]).then_inc(s_out, 16)
                    hq2 = kb.dram_out("d_q1_qkv0", [PART, 8 * (QT + 1)], BF16)
                    p.sync.dma_start(out=hq2, in_=qkvt[0][:]).then_inc(s_out, 16)
                    p.sync.wait(s_out)
                    p.barrier("dbgq1a")
                if getattr(kb, "debug", False) and b == 0 and i == 0 and qtr == 0:
                    for tt in range(3):
                        hq = kb.dram_out(f"d_qkv{tt}", [PART, 8 * (QT + 1)], BF16)
                        p.sync.dma_start(out=hq, in_=qkvt[tt][:]).then_inc(s_out, 16)
                    hl = kb.dram_out("d_lnx1", [PART, 8 * (QT + 1)], BF16)
                    p.sync.dma_start(out=hl, in_=lnx_q[:]).then_inc(s_out, 16)
                    p.sync.wait(s_out)
                    p.barrier("dbg_qkv")
                for tt, cmps in ((0, cmp_q), (1, cmp_k)):
                    tile = qkvt[tt]
                    v3 = tile[:].rearrange("p (s n) -> p s n", s=NCHUNK)
                    p.vector.tensor_tensor(out=cmps[0][:].rearrange("p (s n) -> p s n", s=NCHUNK),
                                           in0=v3[:, :, 1:nt], in1=v3[:, :, 0:QT], op=ALU.subtract)
                    p.gpsimd.wait(s_ve)
                    p.gpsimd.wait(s_ac)
                    c3 = cmps[1][:].rearrange("p (s n) -> p s n", s=NCHUNK)
                    p.gpsimd.tensor_copy(out=c3[64:128, :, :], in_=v3[0:64, :, 1:nt])
                    p.gpsimd.tensor_copy(out=c3[0:64, 1:8, :], in_=v3[64:128, 0:7, 1:nt])
                    p.gpsimd.tensor_copy(out=c3[0:64, 0:1, :], in_=v3[64:128, 7:8, 1:nt]).then_inc(s_gp, 1)
                    cosops(cmps[2][:], tile[:].rearrange("p (s n) -> p s n", s=NCHUNK)[:, :, 1:nt], 8 * QT, qk_l)
                p.vector.wait(s_gp)
                p.barrier(f"l1a{b}_{i}_{qtr}_cmp")
                if getattr(kb, "debug", False) and b == 0 and i == 0 and qtr == 0:
                    for nm3, t3 in (("d_cq1", cmp_q[0]), ("d_cq2", cmp_q[1]), ("d_cq3", cmp_q[2]),
                                    ("d_ck1", cmp_k[0]), ("d_ck2", cmp_k[1]), ("d_ck3", cmp_k[2])):
                        hc3 = kb.dram_out(nm3, [PART, 8 * QT], BF16)
                        p.sync.dma_start(out=hc3, in_=t3[:]).then_inc(s_out, 16)
                    p.sync.wait(s_out)
                    p.barrier("dbg_cmp")
                vq = qkvt[0][:].rearrange("p (s n) -> p s n", s=NCHUNK)[:, :, 1:nt]
                vk = qkvt[1][:].rearrange("p (s n) -> p s n", s=NCHUNK)[:, :, 1:nt]
                vv = qkvt[2][:].rearrange("p (s n) -> p s n", s=NCHUNK)[:, :, 1:nt]
                q0f = carve(182, 2048, BF16)
                k0f = carve(186, 2048, BF16)
                v0f = carve(190, 2048, BF16)
                p.vector.tensor_copy(out=q0f[:].rearrange("p (s n) -> p s n", s=NCHUNK), in_=vq)
                p.vector.tensor_copy(out=k0f[:].rearrange("p (s n) -> p s n", s=NCHUNK), in_=vk)
                p.vector.tensor_copy(out=v0f[:].rearrange("p (s n) -> p s n", s=NCHUNK), in_=vv)
                p.vector.fence(s_ve)
                q_ = [q0f[:], cmp_q[0][:], cmp_q[1][:], cmp_q[2][:]]
                k_ = [k0f[:], cmp_k[0][:], cmp_k[1][:], cmp_k[2][:]]
                qk_terms = [
                    [(0, 0, 1), (1, 1, -1), (2, 2, -1), (3, 3, -1)],
                    [(0, 1, 1), (1, 0, 1), (2, 3, 1), (3, 2, -1)],
                    [(0, 2, 1), (1, 3, -1), (2, 0, 1), (3, 1, 1)],
                    [(0, 3, 1), (1, 2, 1), (2, 1, -1), (3, 0, 1)],
                ]
                n8 = 8 * QT
                for l in range(4):
                    if l == 0:
                        pass
                    elif l == 1:
                        p.vector.tensor_tensor(out=v_l[:].rearrange("p (s n) -> p s n", s=NCHUNK),
                                               in0=vv,
                                               in1=qkvt[2][:].rearrange("p (s n) -> p s n", s=NCHUNK)[:, :, 0:QT],
                                               op=ALU.subtract)
                        p.vector.fence(s_ve)
                    elif l == 2:
                        w3 = v_l[:].rearrange("p (s n) -> p s n", s=NCHUNK)
                        p.vector.fence(s_ve)
                        p.gpsimd.wait(s_ve)
                        p.gpsimd.tensor_copy(out=w3[64:128, :, :], in_=vv[0:64, :, :])
                        p.gpsimd.tensor_copy(out=w3[0:64, 1:8, :], in_=vv[64:128, 0:7, :])
                        p.gpsimd.tensor_copy(out=w3[0:64, 0:1, :], in_=vv[64:128, 7:8, :]).then_inc(s_gp, 1)
                        p.vector.wait(s_gp)
                    else:
                        p.vector.fence(s_ve)
                        cosops(v_l[:], v0f[:], n8, qk_l)
                    first = True
                    for (qi, ki, sg) in qk_terms[l]:
                        if first:
                            assert sg > 0
                            p.vector.tensor_tensor(out=qk_l[:], in0=q_[qi], in1=k_[ki], op=ALU.mult)
                            first = False
                        else:
                            p.vector.tensor_tensor(out=tmp_l[:], in0=q_[qi], in1=k_[ki], op=ALU.mult)
                            p.vector.tensor_tensor(out=qk_l[:], in0=qk_l[:], in1=tmp_l[:],
                                                   op=ALU.add if sg > 0 else ALU.subtract)
                    if getattr(kb, "debug", False) and b == 0 and i == 0 and qtr == 0:
                        p.vector.fence(s_ve)
                        hqk = kb.dram_out(f"d_qkl{l}", [PART, 8 * QT], F32)
                        p.sync.wait(s_ve)
                        p.sync.dma_start(out=hqk, in_=qk_l[:]).then_inc(s_out, 16)
                        hvl = kb.dram_out(f"d_vl{l}", [PART, 8 * QT], F32)
                        p.sync.dma_start(out=hvl, in_=v_l[:]).then_inc(s_out, 16)
                        p.barrier(f"dbg_qkl{l}")
                    p.vector.fence(s_ve)
                    vop = v0f[:] if l == 0 else v_l[:]
                    if l == 0:
                        p.vector.tensor_tensor(out=p0_t[:], in0=qk_l[:], in1=vop, op=ALU.mult)
                    else:
                        p.vector.tensor_tensor(out=tmp_l[:], in0=qk_l[:], in1=vop, op=ALU.mult)
                        p.vector.fence(s_ve)
                        p.vector.tensor_tensor(out=p0_t[:], in0=p0_t[:], in1=tmp_l[:], op=ALU.subtract)
                    p.vector.fence(s_ve)
                p.vector.tensor_copy(out=x0_bf[:], in_=p0_t[:]).then_inc(s_ve, 1)
                p.barrier(f"l1a{b}_{i}_{qtr}_ham")
                if getattr(kb, "debug", False) and b == 0 and i == 0 and qtr == 1:
                    for nm6, t6 in (("d_q1_cq1", cmp_q[0]), ("d_q1_cq2", cmp_q[1]), ("d_q1_cq3", cmp_q[2]),
                                    ("d_q1_qkv1", qkvt[1]), ("d_q1_qkv2", qkvt[2])):
                        hc6 = kb.dram_out(nm6, [PART, t6.shape[-1] if hasattr(t6, "shape") else 8 * QT], BF16)
                        p.sync.dma_start(out=hc6, in_=t6[:]).then_inc(s_out, 16)
                    hp1 = kb.dram_out("d_q1_p0", [PART, 8 * QT], F32)
                    p.sync.dma_start(out=hp1, in_=p0_t[:]).then_inc(s_out, 16)
                    p.sync.wait(s_out)
                    p.barrier("dbgq1b")
                if getattr(kb, "debug", False) and b == 0 and i == 0 and qtr == 0:
                    hq3 = kb.dram_out("d_qk3", [PART, 8 * QT], F32)
                    p.sync.dma_start(out=hq3, in_=qk_l[:]).then_inc(s_out, 16)
                    hv3 = kb.dram_out("d_v3", [PART, 8 * QT], F32)
                    p.sync.dma_start(out=hv3, in_=v_l[:]).then_inc(s_out, 16)
                    p.sync.wait(s_out)
                    p.barrier("dbg_qk3")
                if getattr(kb, "debug", False) and b == 0 and i == 0 and qtr == 0:
                    hp0 = kb.dram_out("d_p0", [PART, 8 * QT], F32)
                    p.sync.dma_start(out=hp0, in_=p0_t[:]).then_inc(s_out, 16)
                    p.sync.wait(s_out)
                    p.barrier("dbg_p0")
                for (mat, dst) in ((A_bf, zr_t), (B_bf, zi_t)):
                    for oc in range(NCHUNK):
                        p.tensor.wait(s_ev)
                        for c in range(NCHUNK):
                            p.tensor.matmul(psA[:, (oc % 4) * 512:(oc % 4) * 512 + QT],
                                            mat[:, (c * 8 + oc) * PART:(c * 8 + oc + 1) * PART],
                                            seg(x0_bf, c, QT),
                                            start=(c == 0), stop=(c == NCHUNK - 1))
                        p.tensor.last().then_inc(s_mm, 1)
                        p.vector.wait(s_mm)
                        p.vector.tensor_copy(out=seg(dst, oc, QT),
                                             in_=psA[:, (oc % 4) * 512:(oc % 4) * 512 + QT]).then_inc(s_ev, 1)
                for c in range(NCHUNK):
                    p.sync.dma_start(out=seg(fc_t, c, QT), in_=fc_filt_h[c * PART:(c + 1) * PART, q0:q0 + QT]).then_inc(s_x, 16)
                    p.sync.dma_start(out=seg(fs_t, c, QT), in_=fs_filt_h[c * PART:(c + 1) * PART, q0:q0 + QT]).then_inc(s_x, 16)
                p.barrier(f"l1a{b}_{i}_{qtr}_z")
                if getattr(kb, "debug", False) and b == 0 and i == 0 and qtr == 0:
                    for nm4, t4, dt4 in (("d_zr", zr_t, F32), ("d_zi", zi_t, F32), ("d_x0", x0_bf, BF16)):
                        hz = kb.dram_out(nm4, [PART, 8 * QT], dt4)
                        p.sync.dma_start(out=hz, in_=t4[:]).then_inc(s_out, 16)
                    p.sync.wait(s_out)
                    p.barrier("dbg_z")
                p.vector.wait(s_x)
                p.vector.tensor_tensor(out=tmp_l[:], in0=fs_t[:], in1=zi_t[:], op=ALU.mult)
                p.vector.tensor_tensor(out=p0_t[:], in0=fc_t[:], in1=zr_t[:], op=ALU.mult)
                p.vector.fence(s_ve)
                p.vector.tensor_tensor(out=wr_bf[:], in0=p0_t[:], in1=tmp_l[:], op=ALU.subtract)
                p.vector.tensor_tensor(out=tmp_l[:], in0=fs_t[:], in1=zr_t[:], op=ALU.mult)
                p.vector.tensor_tensor(out=p0_t[:], in0=fc_t[:], in1=zi_t[:], op=ALU.mult)
                p.vector.fence(s_ve)
                p.vector.tensor_tensor(out=wi_bf[:], in0=p0_t[:], in1=tmp_l[:], op=ALU.add).then_inc(s_ve, 1)
                p.barrier(f"l1a{b}_{i}_{qtr}_w")
                if getattr(kb, "debug", False) and b == 0 and i == 0 and qtr == 0:
                    for nm5, t5 in (("d_wr", wr_bf), ("d_wi", wi_bf)):
                        pass
                    hfc = kb.dram_out("d_fc", [PART, 8 * QT], F32)
                    p.sync.dma_start(out=hfc, in_=fc_t[:]).then_inc(s_out, 16)
                    for nm5, t5 in (("d_wr", wr_bf), ("d_wi", wi_bf)):
                        hw5 = kb.dram_out(nm5, [PART, 8 * QT], BF16)
                        p.sync.dma_start(out=hw5, in_=t5[:]).then_inc(s_out, 16)
                    p.sync.wait(s_out)
                    p.barrier("dbg_w")
                for oc in range(NCHUNK):
                    p.tensor.wait(s_ev)
                    for c in range(NCHUNK):
                        p.tensor.matmul(psA[:, (oc % 4) * 512:(oc % 4) * 512 + QT],
                                        A_bf[:, (c * 8 + oc) * PART:(c * 8 + oc + 1) * PART],
                                        seg(wr_bf, c, QT), start=(c == 0), stop=False)
                    for c in range(NCHUNK):
                        p.tensor.matmul(psA[:, (oc % 4) * 512:(oc % 4) * 512 + QT],
                                        B_bf[:, (c * 8 + oc) * PART:(c * 8 + oc + 1) * PART],
                                        seg(wi_bf, c, QT), start=False, stop=(c == NCHUNK - 1))
                    p.tensor.last().then_inc(s_mm, 1)
                    p.scalar.wait(s_mm)
                    p.scalar.copy(out=seg(a_bfp, oc, QT),
                                  in_=psA[:, (oc % 4) * 512:(oc % 4) * 512 + QT]).then_inc(s_ev, 1)
                p.barrier(f"l1a{b}_{i}_{qtr}_a")
                if getattr(kb, "debug", False) and b == 0 and i == 0 and qtr == 0:
                    ha = kb.dram_out("d_a", [PART, 8 * QT], BF16)
                    p.sync.dma_start(out=ha, in_=a_bfp[:]).then_inc(s_out, 16)
                    p.sync.wait(s_out)
                    p.barrier("dbg_a")
                p.sync.wait(s_mm)
                for c in range(NCHUNK):
                    p.sync.dma_start(out=seg(xl_q, c, QT), in_=xl_src[c * PART:(c + 1) * PART, q0:q0 + QT]).then_inc(s_x, 16)
                p.vector.wait(s_x)
                for oc in range(NCHUNK):
                    p.tensor.wait(s_ev)
                    for c in range(NCHUNK):
                        p.tensor.matmul(psA[:, (oc % 4) * 512:(oc % 4) * 512 + QT],
                                        ap_bf[:, c * D + oc * PART: c * D + (oc + 1) * PART],
                                        seg(a_bfp, c, QT), start=(c == 0), stop=(c == NCHUNK - 1))
                    p.tensor.last().then_inc(s_mm, 1)
                    p.vector.wait(s_mm)
                    p.vector.scalar_tensor_tensor(out=seg(xl_q, oc, QT),
                                                  in0=psA[:, (oc % 4) * 512:(oc % 4) * 512 + QT],
                                                  scalar=pcol(i, B_apb, oc), in1=seg(xl_q, oc, QT),
                                                  op0=ALU.add, op1=ALU.add).then_inc(s_ev, 1)
                p.sync.wait(s_ev)
                for c in range(NCHUNK):
                    p.sync.dma_start(out=xl_dst[c * PART:(c + 1) * PART, q0:q0 + QT], in_=seg(xl_q, c, QT)).then_inc(s_out, 16)
                p.barrier(f"l1a{b}_{i}_{qtr}_pr")
                if getattr(kb, "debug", False) and b == 0 and i == 0 and qtr == 0:
                    hxq = kb.dram_out("d_xlq0", [D, QT], F32)
                    p.sync.dma_start(out=hxq, in_=xl_dst[:, 0:QT]).then_inc(s_out, 16)
                    p.sync.wait(s_out)
                    p.barrier("dbg_xlq0")

            if getattr(kb, "debug", False) and b == 0 and i == 0:
                hba = kb.dram_out("d_afterattn", [D, S], F32)
                p.sync.dma_start(out=hba, in_=xl_dst[:, :]).then_inc(s_out, 16)
                p.sync.wait(s_out)
                p.barrier("dbg_aa")
            p.gpsimd.wait(s_mm)
            for c in range(NCHUNK):
                p.gpsimd.dma_start(out=fc_bf[:, c * 4096:(c + 1) * 4096],
                                   in_=fc_h[i, c * PART:(c + 1) * PART, :]).then_inc(s_w, 16)
            for c in range(32):
                p.gpsimd.dma_start(out=mp_bf[:, c * D:(c + 1) * D],
                                   in_=mp_h[i, c * PART:(c + 1) * PART, :]).then_inc(s_w, 16)
            wB_ready = s_w.n
            for sp in range(NS):
                q0 = sp * SS
                p.sync.wait(s_ve)
                p.sync.wait(s_mm)
                for c in range(NCHUNK):
                    p.sync.dma_start(out=xl_s[:, c * SS:(c + 1) * SS],
                                     in_=xl_dst[c * PART:(c + 1) * PART, q0:q0 + SS]).then_inc(s_x, 16)
                ln_quarter(lambda c: xl_s[:, c * SS:(c + 1) * SS], SS,
                           lambda c: lnx2[:, c * SS:(c + 1) * SS],
                           lambda c: pcol(i, B_ln2w, c), lambda c: pcol(i, B_ln2b, c))
                p.barrier(f"l1a{b}_{i}_{sp}_ln2")
                p.tensor.wait(s_w, wB_ready)
                for j in range(32):
                    p.tensor.wait(s_ac)
                    for c in range(NCHUNK):
                        p.tensor.matmul(psE[:, (j % 2) * 512:(j % 2) * 512 + SS],
                                        fc_bf[:, c * 4096 + j * PART: c * 4096 + (j + 1) * PART],
                                        lnx2[:, c * SS:(c + 1) * SS],
                                        start=(c == 0), stop=(c == NCHUNK - 1))
                    p.tensor.last().then_inc(s_mm, 1)
                    p.scalar.wait(s_mm)
                    p.scalar.activation(out=g_bf2[j % 2][:, 0:SS], in_=psE[:, (j % 2) * 512:(j % 2) * 512 + SS],
                                        func=AF.Gelu, bias=pcol(i, B_fcb, j)).then_inc(s_ac, 1)
                    p.tensor.wait(s_ac)
                    for oc in range(NCHUNK):
                        p.tensor.matmul(psA[:, oc * SS:(oc + 1) * SS],
                                        mp_bf[:, j * D + oc * PART: j * D + (oc + 1) * PART],
                                        g_bf2[j % 2][:, 0:SS],
                                        start=(j == 0 and oc % 2 == 0), stop=(j == 31))
                    p.tensor.last().then_inc(s_mm, 1)
                p.vector.wait(s_mm)
                for oc in range(NCHUNK):
                    p.vector.scalar_tensor_tensor(out=xl_s[:, oc * SS:(oc + 1) * SS],
                                                  in0=psA[:, oc * SS:(oc + 1) * SS],
                                                  scalar=pcol(i, B_mpb, oc),
                                                  in1=xl_s[:, oc * SS:(oc + 1) * SS],
                                                  op0=ALU.add, op1=ALU.add)
                p.vector.last().then_inc(s_ve, 1)
                p.sync.wait(s_ve)
                for c in range(NCHUNK):
                    p.sync.dma_start(out=xl_dst[c * PART:(c + 1) * PART, q0:q0 + SS],
                                     in_=xl_s[:, c * SS:(c + 1) * SS]).then_inc(s_out, 16)
                p.barrier(f"l1a{b}_{i}_{sp}_mlp")
            if getattr(kb, "debug", False) and b == 0 and i == 0:
                hbs = kb.dram_out("d_sub0", [D, S], F32)
                p.sync.dma_start(out=hbs, in_=xl_dst[:, :]).then_inc(s_out, 16)
                p.sync.wait(s_out)
                p.barrier("dbg_s0")

        xbig = fc_bf[:].bitcast(F32)
        xlbig = mp_bf[:].bitcast(F32)
        for c in range(NCHUNK):
            p.sync.wait(s_ve)
            p.sync.wait(s_out)
            p.sync.dma_start(out=xlbig[:, 0:S], in_=xlbufs[cur][c * PART:(c + 1) * PART, :]).then_inc(s_x, 16)
            p.sync.dma_start(out=xbig[:, 0:S], in_=x_hbm[c * PART:(c + 1) * PART, :]).then_inc(s_x, 16)
            p.vector.wait(s_x)
            p.vector.tensor_tensor(out=xlbig[:, 0:S], in0=xlbig[:, 0:S],
                                   in1=xbig[:, 0:S], op=ALU.add).then_inc(s_ve, 1)
            p.sync.wait(s_ve)
            p.sync.dma_start(out=y_hbm[c * PART:(c + 1) * PART, :], in_=xlbig[:, 0:S]).then_inc(s_out, 16)
        p.barrier(f"l1a{b}_fin")
    p.sync.wait(s_out)


def run_l0_attn(x, inputs, debug=False):
    nc = bass.Bass("TRN2", target_bir_lowering=False)
    kb = Kb(nc)
    kb.debug = debug
    build_common(kb)
    xa, ya = [], []
    for b in range(BPC):
        xa.append(kb.dram_in(f"xT{b}", [D, S]))
        ya.append(kb.dram_out(f"yT{b}", [D, S]))
    build_l0_attn(kb, xa, ya)
    prog = kb.p.build()

    com = dict(kb.inputs)
    com["L0_attn_W_qkv"] = np.ascontiguousarray(inputs["L0_attn_W_qkv"], dtype=np.float32)
    com["L0_attn_W_proj"] = np.ascontiguousarray(inputs["L0_attn_W_proj"], dtype=np.float32)
    com["L0_attn_u_qkv_r"] = np.ascontiguousarray(inputs["L0_attn_u_qkv"].reshape(1, -1), dtype=np.float32)
    com["L0_attn_u_proj_r"] = np.ascontiguousarray(inputs["L0_attn_u_proj"].reshape(1, -1), dtype=np.float32)
    com["L0_attn_sigma_qkv_r"] = np.ascontiguousarray(inputs["L0_attn_sigma_qkv"].reshape(1, 1), dtype=np.float32)
    com["L0_attn_sigma_proj_r"] = np.ascontiguousarray(inputs["L0_attn_sigma_proj"].reshape(1, 1), dtype=np.float32)
    in_maps = []
    for core in range(NCORES):
        m = dict(com)
        for b in range(BPC):
            m[f"xT{b}"] = np.ascontiguousarray(x[core * BPC + b].T)
        in_maps.append(m)
    res = run_bass_kernel_spmd(prog, in_maps, core_ids=list(range(NCORES)))
    y = np.empty_like(x)
    for core in range(NCORES):
        for b in range(BPC):
            y[core * BPC + b] = np.asarray(res.results[core][f"yT{b}"]).T
    if debug:
        return y, {nm: np.asarray(res.results[0][nm]) for nm in kb.dbg_names}
    return y


def _prep_l0f(inputs):
    f32 = np.float32
    g = lambda k: np.ascontiguousarray(inputs[k], dtype=f32)
    com = {}
    dwk = g("L0_ffn_dw_kernel").reshape(8, 128, 3).transpose(1, 0, 2).reshape(128, 24)
    dwb = g("L0_ffn_dw_bias").reshape(8, 128).T
    pwb = g("L0_ffn_pw_b").reshape(8, 128).T
    ewb = g("L0_ffn_expert_b").reshape(8, 8, 128).transpose(2, 0, 1).reshape(128, 64)
    owb = g("L0_ffn_out_b").reshape(8, 128).T
    nw = g("L0_ffn_norm_w").reshape(8, 128).T
    ffdb = g("L0_ffn_ff_down_b").reshape(2, 128).T
    ffub = g("L0_ffn_ff_up_b").reshape(8, 128).T
    com["L0f_prm"] = np.ascontiguousarray(
        np.concatenate([dwk, dwb, pwb, ewb, owb, nw, ffdb, ffub], axis=1))
    com["L0f_pwT"] = np.ascontiguousarray(g("L0_ffn_pw_w").T)
    com["L0f_ewT"] = np.ascontiguousarray(g("L0_ffn_expert_w").transpose(0, 2, 1))
    com["L0f_owT"] = np.ascontiguousarray(g("L0_ffn_out_w").T)
    com["L0f_ffdT"] = np.ascontiguousarray(g("L0_ffn_ff_down_w").T)
    com["L0f_ffuT"] = np.ascontiguousarray(g("L0_ffn_ff_up_w").T)
    com["L0f_rwT"] = np.ascontiguousarray(g("L0_ffn_router_w").T)
    com["L0f_rb"] = np.ascontiguousarray(g("L0_ffn_router_b").reshape(1, 8))
    import ml_dtypes
    sel8 = np.zeros((8, D), dtype=f32)
    for e in range(8):
        sel8[e, e * PART:(e + 1) * PART] = 1.0
    com["c_sel8"] = sel8.astype(ml_dtypes.bfloat16)
    com["c_id_f32"] = np.eye(PART, dtype=f32)
    com["c_iota8t"] = _CONSTS["iota8t"]
    return com


def run_l0_ffn(x, inputs, debug=False):
    nc = bass.Bass("TRN2", target_bir_lowering=False)
    kb = Kb(nc)
    kb.debug = debug
    build_common(kb)
    xa, ya = [], []
    for b in range(BPC):
        xa.append(kb.dram_in(f"xT{b}", [D, S]))
        ya.append(kb.dram_out(f"yT{b}", [D, S]))
    build_l0_ffn(kb, xa, ya)
    prog = kb.p.build()
    com = dict(kb.inputs)
    com.update(_prep_l0f(inputs))
    in_maps = []
    for core in range(NCORES):
        m = dict(com)
        for b in range(BPC):
            m[f"xT{b}"] = np.ascontiguousarray(x[core * BPC + b].T)
        in_maps.append(m)
    res = run_bass_kernel_spmd(prog, in_maps, core_ids=list(range(NCORES)))
    y = np.empty_like(x)
    for core in range(NCORES):
        for b in range(BPC):
            y[core * BPC + b] = np.asarray(res.results[core][f"yT{b}"]).T
    if debug:
        dbg = {k: np.asarray(v) for k, v in res.results[0].items() if k.startswith("d_")}
        return y, dbg
    return y




_L1A_CONST_CACHE = {}


def _l1a_consts():
    if _L1A_CONST_CACHE:
        return _L1A_CONST_CACHE
    import ml_dtypes
    Hh, HDd = 16, 64
    wh = np.exp(-2j * np.pi * np.outer(np.arange(Hh), np.arange(Hh)) / Hh)
    wd = np.exp(-2j * np.pi * np.outer(np.arange(HDd), np.arange(HDd)) / HDd)
    W2D = np.kron(wh, wd)
    A = W2D.real.astype(np.float32)
    Bm = W2D.imag.astype(np.float32)
    def tiles(M):
        t = np.empty((PART, 64 * PART), dtype=np.float32)
        for c in range(8):
            for o in range(8):
                t[:, (c * 8 + o) * PART:(c * 8 + o + 1) * PART] = M[c * PART:(c + 1) * PART,
                                                                    o * PART:(o + 1) * PART]
        return t.astype(ml_dtypes.bfloat16)
    _L1A_CONST_CACHE["c_Abf"] = tiles(A)
    _L1A_CONST_CACHE["c_Bbf"] = tiles(Bm)
    ks = np.fft.fftfreq(S)[:, None, None]
    kh = np.fft.fftfreq(Hh)[None, :, None]
    kd = np.fft.fftfreq(HDd)[None, None, :]
    km = np.sqrt(ks ** 2 + kh ** 2 + kd ** 2 + 1e-10)
    th = np.arctan(np.log(km + 1e-10))
    lam = np.arange(S) / S
    t_ = np.arange(S) * 0.1
    mod = np.sin(t_ + lam) * np.cos(t_ - 2.0 * lam + 0.5 * lam ** 2)
    scale = (1.0 + 0.1 * mod) / (Hh * HDd)
    _L1A_CONST_CACHE["c_filtc"] = np.ascontiguousarray(
        (np.cos(th).reshape(S, D) * scale[:, None]).T.astype(np.float32))
    _L1A_CONST_CACHE["c_filts"] = np.ascontiguousarray(
        (np.sin(th).reshape(S, D) * scale[:, None]).T.astype(np.float32))
    return _L1A_CONST_CACHE


def _prep_l1a(inputs):
    f32 = np.float32
    g = lambda k: np.ascontiguousarray(inputs[k], dtype=f32)
    com = dict(_l1a_consts())
    aw = g("L1_attn_attn_w").reshape(4, D, 16, 3, 64).transpose(0, 1, 3, 2, 4).reshape(4, D, 3 * D)
    com["L1a_awT"] = np.ascontiguousarray(aw)
    com["L1a_apT"] = g("L1_attn_aproj_w")
    com["L1a_fcT"] = g("L1_attn_fc_w")
    com["L1a_mpT"] = g("L1_attn_mproj_w")
    ab = g("L1_attn_attn_b").reshape(4, 16, 3, 64).transpose(0, 2, 1, 3).reshape(4, 3 * D)
    prm = np.zeros((PART, 4 * 104), dtype=f32)
    for i in range(4):
        o = i * 104
        prm[:, o + 0:o + 8] = g("L1_attn_ln1_w")[i].reshape(8, 128).T
        prm[:, o + 8:o + 16] = g("L1_attn_ln1_b")[i].reshape(8, 128).T
        prm[:, o + 16:o + 24] = g("L1_attn_ln2_w")[i].reshape(8, 128).T
        prm[:, o + 24:o + 32] = g("L1_attn_ln2_b")[i].reshape(8, 128).T
        prm[:, o + 32:o + 56] = ab[i].reshape(24, 128).T
        prm[:, o + 56:o + 64] = g("L1_attn_aproj_b")[i].reshape(8, 128).T
        prm[:, o + 64:o + 96] = g("L1_attn_fc_b")[i].reshape(32, 128).T
        prm[:, o + 96:o + 104] = g("L1_attn_mproj_b")[i].reshape(8, 128).T
    com["L1a_prm"] = np.ascontiguousarray(prm)
    return com


def run_l1_attn(x, inputs, debug=False):
    nc = bass.Bass("TRN2", target_bir_lowering=False)
    kb = Kb(nc)
    kb.debug = debug
    build_common(kb)
    xa, ya = [], []
    for b in range(BPC):
        xa.append(kb.dram_in(f"xT{b}", [D, S]))
        ya.append(kb.dram_out(f"yT{b}", [D, S]))
    build_l1_attn(kb, xa, ya)
    prog = kb.p.build()
    com = dict(kb.inputs)
    com.update(_prep_l1a(inputs))
    in_maps = []
    for core in range(NCORES):
        m = dict(com)
        for b in range(BPC):
            m[f"xT{b}"] = np.ascontiguousarray(x[core * BPC + b].T)
        in_maps.append(m)
    res = run_bass_kernel_spmd(prog, in_maps, core_ids=list(range(NCORES)))
    y = np.empty_like(x)
    for core in range(NCORES):
        for b in range(BPC):
            y[core * BPC + b] = np.asarray(res.results[core][f"yT{b}"]).T
    if debug:
        return y, {k: np.asarray(v) for k, v in res.results[0].items() if k.startswith("d_")}
    return y




def build_l1_ffn(kb, x_hbm_list, y_hbm_list):
    nc, p = kb.nc, kb.p
    sb, ps = kb.sbuf, kb.psum

    w1_h = kb.dram_in("L1f_W1", [8, D, D])
    w2_h = kb.dram_in("L1f_W2", [8, D, D])
    wg_h = kb.dram_in("L1f_Wg", [D, 8])
    bg_h = kb.dram_in("L1f_bg", [1, 8])
    b1_h = kb.dram_in("L1f_b1", [PART, 64])
    b2_h = kb.dram_in("L1f_b2", [8, D])
    sel8_h = kb.dram_in("c_sel8f", [8, D], BF16)
    iota_h = kb.dram_in("c_iota8t_f", [PART, PART])

    comb = sb("l1f_comb", [PART, NCHUNK * S], F32)
    lnx_bf = sb("l1f_lnx", [PART, NCHUNK * S], BF16)
    g_bf = sb("l1f_g", [PART, NCHUNK * S], BF16)
    wbuf = sb("l1f_wbuf", [PART, NCHUNK * D], BF16)
    maskB = sb("l1f_maskB", [PART, S], BF16)
    mask8 = sb("l1f_mask8", [8, S], BF16)
    wg_f = sb("l1f_wg", [PART, 64], F32)
    b2_sb = sb("l1f_b2sb", [8, D], BF16)
    bg_sb = sb("l1f_bg", [1, 8], F32)
    b1_sb = sb("l1f_b1", [PART, 64], F32)
    sel8 = sb("l1f_sel8", [8, D], BF16)
    iota8t = sb("l1f_iota", [PART, PART], F32)
    idf = sb("l1f_idf", [PART, PART], F32)
    logits = sb("l1f_logits", [PART, PART], F32)
    tsel = sb("l1f_tsel", [PART, PART], F32)
    selp = sb("l1f_selp", [PART, PART], F32)
    m1 = sb("l1f_m1", [PART, 16], F32)
    tww = sb("l1f_tww", [PART, 16], F32)
    tmpP = sb("l1f_tmp", [PART, 512], F32)
    row_a = sb("l1f_rowa", [1, 512], F32)
    row_b = sb("l1f_rowb", [1, 512], F32)
    row_c = sb("l1f_rowc", [1, 512], F32)
    idf_h2 = kb.dram_in("c_id_f32_l1f", [PART, PART])

    psA = ps("l1f_psA", [PART, 2048], F32)
    psC = ps("l1f_psC", [PART, 512], F32)
    psD = ps("l1f_psD", [PART, 512], F32)
    psE = ps("l1f_psE", [PART, 1024], F32)

    s_x = p.sem("l1f_x")
    s_w = p.sem("l1f_w")
    s_mm = p.sem("l1f_mm")
    s_ev = p.sem("l1f_ev")
    s_ve = p.sem("l1f_ve")
    s_ac = p.sem("l1f_ac")
    s_out = p.sem("l1f_out")

    p.sync.dma_start(out=wg_f[:], in_=wg_h.rearrange("(c p) e -> p (c e)", p=PART)) if False else None
    for c in range(NCHUNK):
        p.sync.dma_start(out=wg_f[:, c * 8:(c + 1) * 8], in_=wg_h[c * PART:(c + 1) * PART, :]).then_inc(s_w, 16)
    p.sync.dma_start(out=bg_sb[:], in_=bg_h).then_inc(s_w, 16)
    p.sync.dma_start(out=b1_sb[:], in_=b1_h).then_inc(s_w, 16)
    p.gpsimd.dma_start(out=b2_sb[:], in_=b2_h).then_inc(s_w, 16)
    p.sync.dma_start(out=sel8[:], in_=sel8_h).then_inc(s_w, 16)
    p.sync.dma_start(out=iota8t[:], in_=iota_h).then_inc(s_w, 16)
    p.sync.dma_start(out=idf[:], in_=idf_h2).then_inc(s_w, 16)
    prm_ready = s_w.n

    for b in range(BPC):
        x_hbm = x_hbm_list[b]
        y_hbm = y_hbm_list[b]
        p.sync.wait(s_out)
        for c in range(NCHUNK):
            p.sync.dma_start(out=comb[:, c * S:(c + 1) * S],
                             in_=x_hbm[c * PART:(c + 1) * PART, :]).then_inc(s_x, 16)
        p.tensor.wait(s_x)
        p.tensor.wait(s_w, prm_ready)
        p.tensor.wait(s_ac)
        p.tensor.wait(s_ve)
        p.tensor.wait(s_ev)
        for t in range(4):
            sp = slice(t * 512, (t + 1) * 512)
            for c in range(NCHUNK):
                p.tensor.matmul(psD[0:1, 0:512], kb.ones_col[:], comb[:, c * S + t * 512: c * S + (t + 1) * 512],
                                start=(c == 0), stop=(c == NCHUNK - 1))
            p.tensor.last().then_inc(s_mm, 1)
            for c in range(NCHUNK):
                p.scalar.wait(s_mm)
                p.scalar.activation(out=tmpP[:], in_=comb[:, c * S + t * 512: c * S + (t + 1) * 512],
                                    func=AF.Square).then_inc(s_ac, 1)
                p.tensor.wait(s_ac)
                p.tensor.matmul(psC[0:1, 0:512], kb.ones_col[:], tmpP[:],
                                start=(c == 0), stop=(c == NCHUNK - 1)).then_inc(s_mm, 1)
            p.scalar.wait(s_mm)
            p.scalar.wait(s_ve)
            p.scalar.activation(out=row_a[0:1, :], in_=psD[0:1, 0:512], func=AF.Copy, scale=1.0 / D)
            p.scalar.activation(out=row_b[0:1, :], in_=row_a[0:1, :], func=AF.Square).then_inc(s_ac, 1)
            p.vector.wait(s_ac)
            p.vector.tensor_scalar(out=row_c[0:1, :], in0=psC[0:1, 0:512], scalar1=1.0 / D,
                                   scalar2=None, op0=ALU.mult)
            p.vector.tensor_tensor(out=row_c[0:1, :], in0=row_c[0:1, :], in1=row_b[0:1, :],
                                   op=ALU.subtract).then_inc(s_ve, 1)
            p.scalar.wait(s_ve)
            p.scalar.activation(out=row_b[0:1, :], in_=row_c[0:1, :], func=AF.Ln, bias=kb.eps_col[0:1, :])
            p.scalar.activation(out=row_c[0:1, :], in_=row_b[0:1, :], func=AF.Exp, scale=-0.5).then_inc(s_ac, 1)
            p.vector.wait(s_ac)
            p.vector.scalar_tensor_tensor(out=row_b[0:1, :], in0=row_a[0:1, :], scalar=-1.0,
                                          in1=row_c[0:1, :], op0=ALU.mult, op1=ALU.mult).then_inc(s_ve, 1)
            p.tensor.wait(s_ve)
            p.tensor.matmul(psC[:, 0:512], kb.ones_row[:], row_c[0:1, :], start=True, stop=True)
            p.tensor.matmul(psD[:, 0:512], kb.ones_row[:], row_b[0:1, :],
                            start=True, stop=True).then_inc(s_mm, 1)
            p.vector.wait(s_mm)
            for c in range(NCHUNK):
                p.vector.tensor_tensor(out=tmpP[:], in0=comb[:, c * S + t * 512: c * S + (t + 1) * 512],
                                       in1=psC[:, 0:512], op=ALU.mult)
                p.vector.tensor_tensor(out=tmpP[:], in0=tmpP[:], in1=psD[:, 0:512], op=ALU.add)
                p.vector.fence(s_ve)
                p.vector.tensor_copy(out=lnx_bf[:, c * S + t * 512: c * S + (t + 1) * 512], in_=tmpP[:])
                p.vector.fence(s_ve)
                for ti4 in range(4):
                    p.tensor.wait(s_ve)
                    p.tensor.matmul(psE[:, (t * 4 + ti4) * 8:(t * 4 + ti4) * 8 + 8],
                                    tmpP[:, ti4 * PART:(ti4 + 1) * PART],
                                    wg_f[:, c * 8:(c + 1) * 8],
                                    start=(c == 0 and t == 0 and ti4 == 0),
                                    stop=(c == NCHUNK - 1))
                p.tensor.last().then_inc(s_mm, 1)
                p.vector.wait(s_mm)
            p.barrier(f"l1f{b}_{t}_ln")
        p.vector.wait(s_mm)
        p.vector.tensor_copy(out=logits[:], in_=psE[:, 0:128]).then_inc(s_ev, 1)
        p.tensor.wait(s_ev)
        p.tensor.matmul(psC[:, 0:8], kb.ones_row[:], bg_sb[0:1, :], start=True, stop=True).then_inc(s_mm, 1)
        p.vector.wait(s_mm)
        L3 = logits[:].rearrange("p (a e) -> p a e", e=8)
        bc8 = psC[:, 0:8].rearrange("p (a e) -> p a e", a=1).broadcast_to([PART, 16, 8])
        p.vector.tensor_tensor(out=L3, in0=L3, in1=bc8, op=ALU.add)
        p.vector.fence(s_ve)
        p.vector.reduce_max(out=m1[:], in_=L3, axis=AX.X)
        p.vector.fence(s_ve)
        m1b = m1[:].rearrange("p (a e) -> p a e", e=1).broadcast_to([PART, 16, 8])
        T3 = tsel[:].rearrange("p (a e) -> p a e", e=8)
        p.vector.tensor_tensor(out=T3, in0=L3, in1=m1b, op=ALU.is_ge)
        p.vector.fence(s_ve)
        p.vector.tensor_scalar(out=tsel[:], in0=tsel[:], scalar1=8.0, scalar2=None, op0=ALU.mult)
        p.vector.fence(s_ve)
        p.vector.tensor_tensor(out=tsel[:], in0=tsel[:], in1=iota8t[:], op=ALU.subtract)
        p.vector.fence(s_ve)
        p.vector.reduce_max(out=tww[:], in_=tsel[:].rearrange("p (a e) -> p a e", e=8), axis=AX.X)
        p.vector.fence(s_ve)
        twwb = tww[:].rearrange("p (a e) -> p a e", e=1).broadcast_to([PART, 16, 8])
        p.vector.tensor_tensor(out=selp[:].rearrange("p (a e) -> p a e", e=8),
                               in0=tsel[:].rearrange("p (a e) -> p a e", e=8),
                               in1=twwb, op=ALU.is_ge)
        p.vector.fence(s_ve)
        p.tensor.wait(s_ve)
        p.tensor.wait(s_ev)
        for ti in range(16):
            p.tensor.transpose(psA[0:8, ti * PART:(ti + 1) * PART], selp[:, ti * 8:(ti + 1) * 8], idf[:])
        p.tensor.last().then_inc(s_mm, 1)
        p.vector.wait(s_mm)
        p.vector.tensor_copy(out=mask8[:], in_=psA[0:8, 0:S]).then_inc(s_ev, 1)
        p.barrier(f"l1f{b}_gate")

        ew_marks = [0]
        for e in range(8):
            p.gpsimd.wait(s_mm, ew_marks[-1]) if e >= 1 else p.gpsimd.wait(s_mm)
            for ic in range(NCHUNK):
                p.gpsimd.dma_start(out=wbuf[:, ic * D:(ic + 1) * D],
                                   in_=w1_h[e, ic * PART:(ic + 1) * PART, :]).then_inc(s_w, 16)
            w1_ready = s_w.n
            p.tensor.wait(s_ev)
            for t in range(4):
                p.tensor.matmul(psC[:, 0:512], sel8[:, e * PART:(e + 1) * PART],
                                mask8[:, t * 512:(t + 1) * 512], start=True, stop=True).then_inc(s_mm, 1)
                p.vector.wait(s_mm)
                p.vector.tensor_copy(out=maskB[:, t * 512:(t + 1) * 512], in_=psC[:, 0:512]).then_inc(s_ev, 1)
                p.tensor.wait(s_ev)
            p.vector.fence(s_ve)
            p.tensor.wait(s_w, w1_ready)
            for oc in range(NCHUNK):
                for t in range(4):
                    p.tensor.wait(s_ve)
                    slot = (t % 4) * 512
                    for ic in range(NCHUNK):
                        p.tensor.matmul(psA[:, slot:slot + 512],
                                        wbuf[:, ic * D + oc * PART: ic * D + (oc + 1) * PART],
                                        lnx_bf[:, ic * S + t * 512: ic * S + (t + 1) * 512],
                                        start=(ic == 0), stop=(ic == NCHUNK - 1))
                    p.tensor.last().then_inc(s_mm, 1)
                    p.scalar.wait(s_mm)
                    p.scalar.activation(out=g_bf[:, oc * S + t * 512: oc * S + (t + 1) * 512],
                                        in_=psA[:, slot:slot + 512],
                                        func=AF.Gelu, bias=b1_sb[:, e * 8 + oc: e * 8 + oc + 1]).then_inc(s_ac, 1)
                    p.vector.wait(s_ac)
                    p.vector.tensor_tensor(out=g_bf[:, oc * S + t * 512: oc * S + (t + 1) * 512],
                                           in0=g_bf[:, oc * S + t * 512: oc * S + (t + 1) * 512],
                                           in1=maskB[:, t * 512:(t + 1) * 512], op=ALU.mult).then_inc(s_ve, 1)
            p.barrier(f"l1f{b}_e{e}_g")
            p.gpsimd.wait(s_mm)
            for ic in range(NCHUNK):
                p.gpsimd.dma_start(out=wbuf[:, ic * D:(ic + 1) * D],
                                   in_=w2_h[e, ic * PART:(ic + 1) * PART, :]).then_inc(s_w, 16)
            w2_ready = s_w.n
            p.tensor.wait(s_w, w2_ready)
            for oc in range(NCHUNK):
                for t in range(4):
                    p.tensor.wait(s_ev)
                    slot = (t % 4) * 512
                    for ic in range(NCHUNK):
                        p.tensor.matmul(psA[:, slot:slot + 512],
                                        wbuf[:, ic * D + oc * PART: ic * D + (oc + 1) * PART],
                                        g_bf[:, ic * S + t * 512: ic * S + (t + 1) * 512],
                                        start=(ic == 0), stop=(ic == NCHUNK - 1))
                    p.tensor.last().then_inc(s_mm, 1)
                    p.vector.wait(s_mm)
                    p.vector.tensor_tensor(out=comb[:, oc * S + t * 512: oc * S + (t + 1) * 512],
                                           in0=comb[:, oc * S + t * 512: oc * S + (t + 1) * 512],
                                           in1=psA[:, slot:slot + 512], op=ALU.add).then_inc(s_ev, 1)
            p.barrier(f"l1f{b}_e{e}_w2")
            ew_marks.append(s_mm.n)

        p.tensor.wait(s_ev)
        for oc in range(NCHUNK):
            for t in range(4):
                p.tensor.wait(s_ev)
                slot = (t % 4) * 512
                p.tensor.matmul(psA[:, slot:slot + 512], b2_sb[:, oc * PART:(oc + 1) * PART],
                                mask8[:, t * 512:(t + 1) * 512], start=True, stop=True).then_inc(s_mm, 1)
                p.vector.wait(s_mm)
                p.vector.tensor_tensor(out=comb[:, oc * S + t * 512: oc * S + (t + 1) * 512],
                                       in0=comb[:, oc * S + t * 512: oc * S + (t + 1) * 512],
                                       in1=psA[:, slot:slot + 512], op=ALU.add).then_inc(s_ev, 1)
        p.sync.wait(s_ev)
        for c in range(NCHUNK):
            p.sync.dma_start(out=y_hbm[c * PART:(c + 1) * PART, :],
                             in_=comb[:, c * S:(c + 1) * S]).then_inc(s_out, 16)
        p.barrier(f"l1f{b}_end")
    p.sync.wait(s_out)


def _prep_l1f(inputs):
    import ml_dtypes
    f32 = np.float32
    g = lambda k: np.ascontiguousarray(inputs[k], dtype=f32)
    com = {}
    com["L1f_W1"] = g("L1_ffn_W1")
    com["L1f_W2"] = g("L1_ffn_W2")
    com["L1f_Wg"] = g("L1_ffn_Wg")
    com["L1f_bg"] = g("L1_ffn_bg").reshape(1, 8)
    com["L1f_b1"] = np.ascontiguousarray(g("L1_ffn_b1").reshape(8, 8, 128).transpose(2, 0, 1).reshape(128, 64))
    com["L1f_b2"] = g("L1_ffn_b2")
    sel8 = np.zeros((8, D), dtype=f32)
    for e in range(8):
        sel8[e, e * PART:(e + 1) * PART] = 1.0
    com["c_sel8f"] = sel8.astype(ml_dtypes.bfloat16)
    com["c_iota8t_f"] = _CONSTS["iota8t"]
    com["c_id_f32_l1f"] = np.eye(PART, dtype=f32)
    return com


def run_l1_ffn(x, inputs):
    nc = bass.Bass("TRN2", target_bir_lowering=False)
    kb = Kb(nc)
    build_common(kb)
    xa, ya = [], []
    for b in range(BPC):
        xa.append(kb.dram_in(f"xT{b}", [D, S]))
        ya.append(kb.dram_out(f"yT{b}", [D, S]))
    build_l1_ffn(kb, xa, ya)
    prog = kb.p.build()
    com = dict(kb.inputs)
    com.update(_prep_l1f(inputs))
    in_maps = []
    for core in range(NCORES):
        m = dict(com)
        for b in range(BPC):
            m[f"xT{b}"] = np.ascontiguousarray(x[core * BPC + b].T)
        in_maps.append(m)
    res = run_bass_kernel_spmd(prog, in_maps, core_ids=list(range(NCORES)))
    y = np.empty_like(x)
    for core in range(NCORES):
        for b in range(BPC):
            y[core * BPC + b] = np.asarray(res.results[core][f"yT{b}"]).T
    return y


def kernel(**inputs):
    x = np.ascontiguousarray(inputs["L0_attn_batch"], dtype=np.float32)
    x = run_l0_attn(x, inputs)
    x = run_l0_ffn(x, inputs)
    x = run_l1_attn(x, inputs)
    x = run_l1_ffn(x, inputs)
    return x
